# version 02
# baseline (speedup 1.0000x reference)
import os
os.environ.setdefault("TRN_ALLOW_ICI_ALL_TO_ALL", "1")
import numpy as np
import ml_dtypes
import concourse.bass as bass
import concourse.mybir as mybir
from concourse.bass_isa import ReduceOp
from concourse.bass_utils import run_bass_kernel_spmd
from contextlib import ExitStack

F32 = mybir.dt.float32
BF16 = mybir.dt.bfloat16
I32 = mybir.dt.int32
AF = mybir.ActivationFunctionType
ALU = mybir.AluOpType
AX = mybir.AxisListType
BF = ml_dtypes.bfloat16

D = 2048
NK = 16
TOK = 1024
NSEQ = 4096
NT = 8
LN_EPS = 1e-5

DEBUG = False
BUILD_STOP = 99


class _StopRecord(Exception):
    pass


class Prog:
    ENGINES = ("sync", "scalar", "vector", "tensor", "gpsimd")
    EPROG = {"tensor": "T", "vector": "V", "scalar": "S", "gpsimd": "G"}
    EDMA = {"sync": "DS", "scalar": "DA", "gpsimd": "DG", "vector": "DV"}

    def __init__(self, nc, stack):
        self.nc = nc
        self.items = {e: [] for e in self.ENGINES}
        self.sem = {}
        self.cnt = {}
        for nm in ("T", "V", "S", "G", "DS", "DA", "DG", "DV", "CC"):
            self.sem[nm] = stack.enter_context(nc.semaphore("sem_" + nm))
            self.cnt[nm] = 0

    def wait(self, eng, nm, val):
        if val > 0:
            self.items[eng].append(("w", nm, val))

    def run(self, eng, fn):
        self.items[eng].append(("op", fn))

    def sig(self, eng, k=1):
        nm = self.EPROG[eng]
        self.cnt[nm] += k
        self.items[eng].append(("sig", nm, k))
        return self.cnt[nm]

    def dma(self, eng, fn):
        nm = self.EDMA[eng]
        self.cnt[nm] += 16
        self.items[eng].append(("dma", fn, nm))
        return self.cnt[nm]

    def cc(self, fn):
        self.cnt["CC"] += 1
        self.items["gpsimd"].append(("cc", fn))
        return self.cnt["CC"]

    def emit(self, blk):
        for ename in self.ENGINES:
            items = self.items[ename]

            def body(eng, items=items):
                for it in items:
                    if it[0] == "w":
                        eng.wait_ge(self.sem[it[1]], it[2])
                    elif it[0] == "op":
                        it[1](eng)
                    elif it[0] == "sig":
                        eng.drain().then_inc(self.sem[it[1]], it[2])
                    elif it[0] == "dma":
                        it[1](eng).then_inc(self.sem[it[2]], 16)
                    elif it[0] == "cc":
                        it[1](eng).then_inc(self.sem["CC"], 1)

            getattr(blk, ename)(body)


SB_BASE = 16512


class Mem:

    def __init__(self, nc, limit=212800):
        self.nc = nc
        self.limit = limit
        self.names = {}

    def at(self, name, off, shape, dtype):
        esz = 2 if dtype in (BF16,) else 4
        nbytes = int(np.prod(shape[1:])) * esz if len(shape) > 1 else esz
        assert off + nbytes <= self.limit, (name, off, nbytes)
        assert off % 32 == 0, (name, off)
        self.names[name] = (off, nbytes)
        return self.nc.alloc_sbuf_tensor_at(name, list(shape), dtype,
                                            offset=SB_BASE + off)


def build_program(core_dbg=False, stop=None, race_check=True):
    if stop is None:
        stop = BUILD_STOP
    nc = bass.Bass("TRN2", target_bir_lowering=False,
                   detect_race_conditions=race_check)

    xT = nc.dram_tensor("xT", [D, TOK], F32, kind="ExternalInput").ap()
    wfT = nc.dram_tensor("wfT", [D, D], F32, kind="ExternalInput").ap()
    wqkvT = nc.dram_tensor("wqkvT", [D, 1536], F32, kind="ExternalInput").ap()
    wabsp = nc.dram_tensor("wabsp", [D, 128], F32, kind="ExternalInput").ap()
    wgT = nc.dram_tensor("wgT", [D, 512], F32, kind="ExternalInput").ap()
    woT = nc.dram_tensor("woT", [D, D], F32, kind="ExternalInput").ap()
    cw48 = nc.dram_tensor("cw48", [128, 48], F32, kind="ExternalInput").ap()
    gcols_d = nc.dram_tensor("gcols_d", [128, 16], F32, kind="ExternalInput").ap()
    bcols_d = nc.dram_tensor("bcols_d", [128, 16], F32, kind="ExternalInput").ap()
    fbcols_d = nc.dram_tensor("fbcols_d", [128, 16], F32, kind="ExternalInput").ap()
    babcol_d = nc.dram_tensor("babcol_d", [128, 1], F32, kind="ExternalInput").ap()
    coreoff_d = nc.dram_tensor("coreoff_d", [1, 1], I32, kind="ExternalInput").ap()

    y_fm = nc.dram_tensor("y_fm", [D, TOK], F32, kind="ExternalOutput").ap()
    if core_dbg:
        dbg = nc.dram_tensor("dbg", [D, TOK], F32, kind="ExternalOutput").ap()
        dbg2 = nc.dram_tensor("dbg2", [D, TOK], F32, kind="ExternalOutput").ap()
        dbg_xb = nc.dram_tensor("dbg_xb", [D, TOK], F32, kind="ExternalOutput").ap()
        dbg_wg = nc.dram_tensor("dbg_wg", [D, D], F32, kind="ExternalOutput").ap()
        dbg_sb = nc.dram_tensor("dbg_sb", [128, 2 * TOK + 32], F32, kind="ExternalOutput").ap()

    ag1_in = nc.dram_tensor("ag1_in", [D, TOK], BF16).ap()
    ag1_out = nc.dram_tensor("ag1_out", [4, D, TOK], BF16).ap()
    ag2_in = nc.dram_tensor("ag2_in", [NSEQ, 512], BF16).ap()
    ag2_out = nc.dram_tensor("ag2_out", [4, NSEQ, 512], BF16).ap()
    xmid_scr = nc.dram_tensor("xmid_scr", [D, TOK], F32).ap()
    bc_scr = nc.dram_tensor("bc_scr", [1, 2 * TOK + 32], F32).ap()
    wgam_scr = nc.dram_tensor("wgam_scr", [D, D], BF16).ap()

    with ExitStack() as stack:
        P = Prog(nc, stack)
        M = Mem(nc)
        _tailref = {}

        def _ck(n):
            if stop <= n:
                raise _StopRecord()

        CO = 188416
        def cst(name, shape, dtype):
            nonlocal CO
            h = M.at(name, CO, shape, dtype)
            esz = 2 if dtype == BF16 else 4
            nb = int(np.prod(shape[1:])) * esz
            CO += (nb + 31) & ~31
            return h
        ident = cst("ident", [128, 128], BF16)
        strictL = cst("strictL", [128, 128], BF16)
        inclU = cst("inclU", [128, 128], BF16)
        ones_r_bf = cst("ones_r_bf", [1, 128], BF16)
        ones_r_f = cst("ones_r_f", [1, 128], F32)
        ones_c_bf = cst("ones_c_bf", [128, 1], BF16)
        ones_c_f = cst("ones_c_f", [128, 1], F32)
        iota_i = cst("iota_i", [128, 128], I32)
        gcols = cst("gcols", [128, 16], F32)
        bcols_bf = cst("bcols_bf", [128, 16], BF16)
        fb2 = cst("fb2", [128, 16], F32)
        babcol = cst("babcol", [128, 1], F32)
        cwc = cst("cwc", [128, 48], F32)
        d_col = cst("d_col", [128, 16], F32)
        rg_col = cst("rg_col", [128, 16], F32)
        thr2 = cst("thr2", [128, 2], F32)
        ABSC = cst("ABSC", [128, 17], F32)
        ABS16 = cst("ABS16", [128, 16], F32)
        ABS64 = cst("ABS64", [128, 64], F32)
        assert CO <= 191488, CO
        statrow = nc.alloc_sbuf_tensor_at("statrow", [1, 4096], F32,
                                          offset=SB_BASE + 191552)
        zeros128 = nc.alloc_sbuf_tensor_at("zeros128", [128, 128], F32,
                                           offset=SB_BASE + 211008)
        ROW_SUM, ROW_SQ, ROW_T1, ROW_T2 = 0, 1024, 2048, 3072
        SB_SN, SB_UP = 0, 1024

        X = M.at("X", 0, [128, NK * TOK], F32)
        _tailref["X"] = X
        XB = M.at("XB", 65536, [128, NK * TOK], BF16)
        WGAM = M.at("WGAM", 98304, [128, NK * D], BF16)
        XN1 = nc.alloc_sbuf_tensor_at("XN1", [128, NK * TOK], BF16,
                                      offset=SB_BASE + 98304)
        WFA = M.at("WFA", 163840, [128, D], F32)
        WFB = nc.alloc_sbuf_tensor_at("WFB", [128, D], F32,
                                      offset=SB_BASE + 163840 + 8192)
        WFR = [WFA, WFB]
        QSC = M.at("QSC", 180224, [128, D], BF16)
        X2R = nc.alloc_sbuf_tensor_at("X2R", [128, 2 * TOK], BF16,
                                      offset=SB_BASE + 180224)
        SBC = nc.alloc_sbuf_tensor_at("SBC", [128, TOK], F32,
                                      offset=SB_BASE + 163840 + 8192)
        UBC = nc.alloc_sbuf_tensor_at("UBC", [128, TOK], F32,
                                      offset=SB_BASE + 163840 + 12288)

        try:
            P.run("gpsimd", lambda g: g.iota(iota_i[:], [[-1, 128]], base=0,
                                             channel_multiplier=1))
            vG = P.sig("gpsimd")

            def c_masks(v):
                v.tensor_single_scalar(ident[:], iota_i[:], 0, ALU.is_equal)
                v.tensor_single_scalar(strictL[:], iota_i[:], 0, ALU.is_gt)
                v.tensor_single_scalar(inclU[:], iota_i[:], 0, ALU.is_le)
                v.memset(ones_r_bf[:], 1.0)
                v.memset(ones_r_f[:], 1.0)
                v.memset(ones_c_bf[:], 1.0)
                v.memset(ones_c_f[:], 1.0)
                v.memset(zeros128[:], 0.0)
            P.wait("vector", "G", vG)
            P.run("vector", c_masks)
            vconsts = P.sig("vector")

            P.dma("sync", lambda e: e.dma_start(
                out=X[:].rearrange("p (k t) -> p k t", k=NK),
                in_=xT.rearrange("(k p) t -> p k t", p=128)))
            dX = P.cnt["DS"]
            P.dma("sync", lambda e: e.dma_start(out=gcols[:], in_=gcols_d))
            P.dma("sync", lambda e: e.dma_start(out=fb2[:], in_=fbcols_d))
            P.dma("sync", lambda e: e.dma_start(out=babcol[:], in_=babcol_d))
            P.dma("sync", lambda e: e.dma_start(out=cwc[:], in_=cw48))
            coreoff_sb = nc.alloc_sbuf_tensor_at("coreoff_sb", [1, 1], I32,
                                                 offset=SB_BASE + 212032)
            P.dma("sync", lambda e: e.dma_start(out=coreoff_sb[:], in_=coreoff_d))
            dSmall = P.cnt["DS"]
            P.dma("gpsimd", lambda e: e.dma_start(
                out=XB[:].rearrange("p (k t) -> p k t", k=NK),
                in_=xT.rearrange("(k p) t -> p k t", p=128)))
            dXB = P.cnt["DG"]
            P.dma("gpsimd", lambda e: e.dma_start(out=bcols_bf[:], in_=bcols_d))
            dBC = P.cnt["DG"]

            _ck(10)
            vred = {}
            dWF = {}
            for k in range(16):
                if k >= 1:
                    P.wait("sync", "V", vred[k - 1])

                def d_wf(e, k=k):
                    return e.dma_start(out=WFR[k % 2][:],
                                       in_=wfT[128 * k:128 * k + 128, :])
                dWF[k] = P.dma("sync", d_wf)
                P.wait("vector", "DS", dWF[k])

                def redk(v, k=k):
                    for j in range(4):
                        v.tensor_reduce(ABS64[:, 4 * k + j:4 * k + j + 1],
                                        WFR[k % 2][:, 512 * j:512 * j + 512],
                                        AX.X, ALU.add, apply_absolute_value=True)
                P.run("vector", redk)
                vred[k] = P.sig("vector")

            P.run("vector", lambda v: v.tensor_reduce(
                ABSC[:, 16:17], ABS64[:], AX.X, ALU.add))
            vabs = P.sig("vector")

            P.wait("sync", "V", vabs)
            dGat = P.dma("sync", lambda e: e.dma_start(
                out=statrow[0:1, 128:256], in_=ABSC[:, 16:17]))
            P.wait("vector", "DS", dGat)
            P.run("vector", lambda v: v.tensor_reduce(
                statrow[0:1, 256:257], statrow[0:1, 128:256], AX.X, ALU.add))
            vTot = P.sig("vector")

            def s_thr(s):
                s.activation(statrow[0:1, 0:1], statrow[0:1, 256:257],
                             AF.Identity, scale=0.5 / (D * D))
            P.wait("scalar", "V", vTot)
            P.run("scalar", s_thr)
            sThr = P.sig("scalar")

            def v_nthr(v):
                v.tensor_single_scalar(statrow[0:1, 0:1], statrow[0:1, 0:1],
                                       0.5e-5, ALU.add)
                v.drain()
                v.tensor_single_scalar(statrow[0:1, 1:2], statrow[0:1, 0:1],
                                       -1.0, ALU.mult)
            P.wait("vector", "S", sThr)
            P.run("vector", v_nthr)
            vNthr = P.sig("vector")

            P.wait("sync", "V", vNthr)
            dtw = P.dma("sync", lambda e: e.dma_start(
                out=bc_scr[0:1, 2 * TOK:2 * TOK + 2], in_=statrow[0:1, 0:2]))
            P.wait("sync", "DS", dtw)
            dThr = P.dma("sync", lambda e: e.dma_start(
                out=thr2[:], in_=bass.AP(bc_scr.tensor, 2 * TOK, [[0, 128], [1, 2]])))

            _ck(20)
            with nc.psum_tensor("ps_wb", [1, D], F32) as ps_wb, \
                 nc.psum_tensor("ps_rg", [1, D], F32) as ps_rg:
                gq = {}
                tQS = 0
                for k in range(16):
                    if k >= 1:
                        P.wait("sync", "V", gq[k - 1])

                    def d_wf2(e, k=k):
                        return e.dma_start(out=WFR[k % 2][:],
                                           in_=wfT[128 * k:128 * k + 128, :])
                    dQk = P.dma("sync", d_wf2)
                    src = WFR[k % 2][:]
                    P.wait("vector", "DS", dQk)
                    if k == 0:
                        P.wait("vector", "DS", dSmall)
                        P.wait("vector", "DS", dThr)
                        P.wait("vector", "DG", dBC)
                    if tQS:
                        P.wait("vector", "T", tQS)

                    def v_isgt(v, src=src):
                        v.tensor_single_scalar(QSC[:], src, thr2[:, 0:1], ALU.is_gt)
                    P.run("vector", v_isgt)
                    vq = P.sig("vector")

                    def g_stt(v, src=src):
                        v.scalar_tensor_tensor(QSC[:], src, thr2[:, 1:2], QSC[:],
                                               ALU.is_lt, ALU.subtract)
                    P.run("vector", g_stt)
                    gq[k] = P.sig("vector")

                    P.wait("tensor", "V", gq[k])

                    def mm_wb(t, k=k):
                        for j in range(4):
                            t.matmul(ps_wb[:, 512 * j:512 * j + 512],
                                     bcols_bf[:, k:k + 1],
                                     QSC[:, 512 * j:512 * j + 512],
                                     start=(k == 0), stop=(k == 15))
                        t.nop()
                    P.run("tensor", mm_wb)
                    tQS = P.sig("tensor")

                    def v_fold(v, k=k):
                        v.tensor_scalar_mul(WGAM[:, D * k:D * k + D], QSC[:],
                                            gcols[:, k:k + 1])
                    P.run("vector", v_fold)
                    vwg = P.sig("vector")

                    P.wait("tensor", "V", vwg)

                    def mm_rg(t, k=k):
                        for j in range(4):
                            t.matmul(ps_rg[:, 512 * j:512 * j + 512], ones_c_bf[:],
                                     WGAM[:, D * k + 512 * j:D * k + 512 * j + 512],
                                     start=(k == 0), stop=(k == 15))
                        t.nop()
                    P.run("tensor", mm_rg)
                tWB = P.sig("tensor")

                P.wait("sync", "T", tWB)
                dWG = P.dma("sync", lambda e: e.dma_start(
                    out=wgam_scr.rearrange("(k p) o -> p k o", p=128),
                    in_=WGAM[:].rearrange("p (k o) -> p k o", k=NK)))

                P.wait("vector", "T", tWB)
                P.run("vector", lambda v: v.tensor_copy(
                    statrow[0:1, ROW_SUM:ROW_SUM + 2048], ps_wb[0:1, :]))
                vWBr = P.sig("vector")
                P.run("vector", lambda v: v.tensor_copy(
                    statrow[0:1, ROW_T1:ROW_T1 + 2048], ps_rg[0:1, :]))
                vRGr = P.sig("vector")

            with nc.psum_tensor("ps_c16", [128, 32], F32) as ps_c16:
                def mm_cols(t):
                    for k in range(16):
                        t.matmul(ps_c16[:, k:k + 1],
                                 statrow[0:1, ROW_SUM + 128 * k:ROW_SUM + 128 * k + 128],
                                 ones_r_f[0:1, 0:1], start=True, stop=True)
                        t.matmul(ps_c16[:, 16 + k:16 + k + 1],
                                 statrow[0:1, ROW_T1 + 128 * k:ROW_T1 + 128 * k + 128],
                                 ones_r_f[0:1, 0:1], start=True, stop=True)
                    t.nop()
                P.wait("tensor", "V", vRGr)
                P.run("tensor", mm_cols)
                tCols = P.sig("tensor")

                def v_cols(v):
                    v.tensor_scalar(fb2[:], fb2[:], 0.5, None, ALU.mult)
                    v.drain()
                    v.scalar_tensor_tensor(d_col[:], ps_c16[:, 0:16], -0.5, fb2[:],
                                           ALU.mult, ALU.add)
                    v.tensor_copy(rg_col[:], ps_c16[:, 16:32])
                P.wait("vector", "T", tCols)
                P.run("vector", v_cols)
                vDC = P.sig("vector")

            _ck(30)
            def ln_stats(tag, sq_wait_sem, sq_wait_vals, sum_rhs, inner_ln,
                         sum_wait=None, first_sq_waits=(), guard_v=0, xsrc=None):
                with nc.psum_tensor("ps_s" + tag, [1, TOK], F32) as ps_s, \
                     nc.psum_tensor("ps_q" + tag, [1, TOK], F32) as ps_q:
                    tprev = 0
                    for k in range(16):
                        P.wait("scalar", sq_wait_sem, sq_wait_vals[k])
                        if k == 0:
                            for (snm, sval) in first_sq_waits:
                                P.wait("scalar", snm, sval)
                        if tprev:
                            P.wait("scalar", "T", tprev)

                        def s_sq(s, k=k, xs=xsrc):
                            xs = X if xs is None else xs
                            s.activation(X2R[:, (k % 2) * TOK:(k % 2) * TOK + TOK],
                                         xs[:, TOK * k:TOK * k + TOK], AF.Square)
                        P.run("scalar", s_sq)
                        sq_k = P.sig("scalar")

                        P.wait("tensor", "S", sq_k)
                        if sum_wait is not None:
                            P.wait("tensor", sum_wait[0], sum_wait[1][k])

                        def mm_st(t, k=k):
                            for j in range(2):
                                t.matmul(ps_s[:, 512 * j:512 * j + 512], ones_c_bf[:],
                                         sum_rhs(k, j), start=(k == 0), stop=(k == 15))
                                t.matmul(ps_q[:, 512 * j:512 * j + 512], ones_c_bf[:],
                                         X2R[:, (k % 2) * TOK + 512 * j:
                                              (k % 2) * TOK + 512 * j + 512],
                                         start=(k == 0), stop=(k == 15))
                            t.nop()
                        P.run("tensor", mm_st)
                        tprev = P.sig("tensor")

                    def v_stat(v):
                        v.tensor_copy(statrow[0:1, ROW_SUM:ROW_SUM + TOK], ps_s[0:1, :])
                        v.tensor_copy(statrow[0:1, ROW_SQ:ROW_SQ + TOK], ps_q[0:1, :])
                        v.tensor_mul(statrow[0:1, ROW_T1:ROW_T1 + TOK],
                                     statrow[0:1, ROW_SUM:ROW_SUM + TOK],
                                     statrow[0:1, ROW_SUM:ROW_SUM + TOK])
                        if inner_ln:
                            k1 = (1.0 + LN_EPS) / D
                            eps2 = LN_EPS + LN_EPS * LN_EPS
                        else:
                            k1 = 1.0 / D
                            eps2 = LN_EPS
                        v.tensor_scalar(statrow[0:1, ROW_T2:ROW_T2 + TOK],
                                        statrow[0:1, ROW_SQ:ROW_SQ + TOK], k1, eps2,
                                        ALU.mult, ALU.add)
                        v.scalar_tensor_tensor(statrow[0:1, ROW_T2:ROW_T2 + TOK],
                                               statrow[0:1, ROW_T1:ROW_T1 + TOK],
                                               -k1 / D,
                                               statrow[0:1, ROW_T2:ROW_T2 + TOK],
                                               ALU.mult, ALU.add)
                        v.reciprocal(statrow[0:1, ROW_T1:ROW_T1 + TOK],
                                     statrow[0:1, ROW_T2:ROW_T2 + TOK])
                    P.wait("vector", "T", tprev)
                    P.run("vector", v_stat)
                    vst = P.sig("vector")

                def s_stat(s):
                    s.activation(statrow[0:1, ROW_T2:ROW_T2 + TOK],
                                 statrow[0:1, ROW_T1:ROW_T1 + TOK], AF.Sqrt)
                    s.activation(statrow[0:1, ROW_SQ:ROW_SQ + TOK],
                                 statrow[0:1, ROW_T2:ROW_T2 + TOK], AF.Identity,
                                 scale=(-0.5 if inner_ln else 1.0))
                P.wait("scalar", "V", vst)
                P.run("scalar", s_stat)
                sst = P.sig("scalar")

                def v_statb(v):
                    v.tensor_mul(statrow[0:1, ROW_T1:ROW_T1 + TOK],
                                 statrow[0:1, ROW_SUM:ROW_SUM + TOK],
                                 statrow[0:1, ROW_T2:ROW_T2 + TOK])
                    v.tensor_scalar(statrow[0:1, ROW_T1:ROW_T1 + TOK],
                                    statrow[0:1, ROW_T1:ROW_T1 + TOK],
                                    (0.5 / D if inner_ln else -1.0 / D), None,
                                    ALU.mult)
                P.wait("vector", "S", sst)
                P.run("vector", v_statb)
                vstb = P.sig("vector")

                P.wait("sync", "V", vstb)
                P.dma("sync", lambda e: e.dma_start(
                    out=bc_scr[0:1, 0:TOK], in_=statrow[0:1, ROW_SQ:ROW_SQ + TOK]))
                dw2 = P.dma("sync", lambda e: e.dma_start(
                    out=bc_scr[0:1, TOK:2 * TOK],
                    in_=statrow[0:1, ROW_T1:ROW_T1 + TOK]))
                P.wait("sync", "DS", dw2)
                if guard_v:
                    P.wait("sync", "V", guard_v)
                P.dma("sync", lambda e: e.dma_start(
                    out=SBC[:], in_=bass.AP(bc_scr.tensor, 0, [[0, 128], [1, TOK]])))
                dbc = P.dma("sync", lambda e: e.dma_start(
                    out=UBC[:], in_=bass.AP(bc_scr.tensor, TOK, [[0, 128], [1, TOK]])))
                return dbc

            def ffn(tag, wait_T_extra=0, wgam=None, xb=None, xdst=None):
                wgam = WGAM if wgam is None else wgam
                xb = XB if xb is None else xb
                xdst = X if xdst is None else xdst
                with nc.psum_tensor("ps_f" + tag, [128, 8 * 512], F32) as ps_f:
                    vEv = 0
                    for t2 in range(2):
                        for og in range(2):
                            if vEv:
                                P.wait("tensor", "V", vEv)
                            if wait_T_extra and t2 == 0 and og == 0:
                                pass
                            for o in range(8):
                                oc = og * 8 + o

                                def mm_f(t, t2=t2, oc=oc, o=o, wgam=wgam, xb=xb):
                                    for k in range(16):
                                        t.matmul(
                                            ps_f[:, 512 * o:512 * o + 512],
                                            wgam[:, D * k + 128 * oc:D * k + 128 * oc + 128],
                                            xb[:, TOK * k + 512 * t2:TOK * k + 512 * t2 + 512],
                                            start=(k == 0), stop=(k == 15))
                                    t.nop()
                                P.run("tensor", mm_f)
                                tF = P.sig("tensor")

                                P.wait("vector", "T", tF)

                                def v_ep(v, t2=t2, oc=oc, o=o, xdst=xdst):
                                    ts = 512 * t2
                                    pslc = ps_f[:, 512 * o:512 * o + 512]
                                    xs = xdst[:, TOK * oc + ts:TOK * oc + ts + 512]
                                    t1 = WFA[:, 0:512]
                                    v.tensor_mul(t1, pslc, SBC[:, ts:ts + 512])
                                    v.scalar_tensor_tensor(
                                        t1, UBC[:, ts:ts + 512], rg_col[:, oc:oc + 1],
                                        t1, ALU.mult, ALU.add)
                                    v.scalar_tensor_tensor(
                                        xs, t1, d_col[:, oc:oc + 1], xs, ALU.add, ALU.add)
                                P.run("vector", v_ep)
                                vEv = P.sig("vector")
                    return vEv

            vBC1 = ln_stats("1", "DS", {k: dX for k in range(16)},
                            lambda k, j: XB[:, TOK * k + 512 * j:TOK * k + 512 * j + 512],
                            inner_ln=True,
                            sum_wait=("DG", {k: dXB for k in range(16)}),
                            first_sq_waits=(("T", tCols), ("V", vDC)))
            _ck(40)
            if core_dbg:
                P.wait("sync", "V", vDC)
                P.dma("sync", lambda e: e.dma_start(
                    out=dbg_sb[0:1, 0:2], in_=thr2[0:1, 0:2]))
                P.dma("sync", lambda e: e.dma_start(
                    out=dbg_sb[1:3, 0:2], in_=thr2[100:102, 0:2]))
                P.wait("gpsimd", "DS", dWG)
                P.dma("gpsimd", lambda e: e.dma_start(out=dbg_wg, in_=wgam_scr))
                P.wait("gpsimd", "DS", vBC1)
                dDump = P.dma("gpsimd", lambda e: e.dma_start(
                    out=dbg_xb.rearrange("(k p) t -> p k t", p=128),
                    in_=XB[:].rearrange("p (k t) -> p k t", k=NK)))
                P.wait("vector", "DG", dDump)
            P.wait("tensor", "DG", dXB)
            P.wait("tensor", "V", vDC)
            P.wait("tensor", "DS", vBC1)
            vFFN1 = ffn("1")

            _ck(50)
            P.wait("sync", "V", vFFN1)
            P.dma("sync", lambda e: e.dma_start(
                out=xmid_scr.rearrange("(k p) t -> p k t", p=128),
                in_=X[:].rearrange("p (k t) -> p k t", k=NK)))
            if core_dbg:
                P.dma("sync", lambda e: e.dma_start(
                    out=dbg.rearrange("(k p) t -> p k t", p=128),
                    in_=X[:].rearrange("p (k t) -> p k t", k=NK)))

            vxb2 = {}
            for k in range(16):
                if k == 0:
                    P.wait("vector", "V", vFFN1)

                def v_xb2(v, k=k):
                    v.tensor_copy(XB[:, TOK * k:TOK * k + TOK], X[:, TOK * k:TOK * k + TOK])
                P.run("vector", v_xb2)
                vxb2[k] = P.sig("vector")

            sq_waits2 = {k: vFFN1 for k in range(16)}
            vBC2 = ln_stats("2", "V", sq_waits2,
                            lambda k, j: XB[:, TOK * k + 512 * j:TOK * k + 512 * j + 512],
                            inner_ln=False,
                            sum_wait=("V", vxb2), guard_v=vFFN1)
            vXN = 0
            for k in range(16):
                if k == 0:
                    P.wait("vector", "DS", vBC2)
                    P.wait("vector", "DS", dWG)

                def v_xn(v, k=k):
                    t1 = WFA[:, 0:TOK]
                    v.tensor_mul(t1, X[:, TOK * k:TOK * k + TOK], SBC[:])
                    v.tensor_add(XN1[:, TOK * k:TOK * k + TOK], t1, UBC[:])
                P.run("vector", v_xn)
                vXN = P.sig("vector")

            _ck(60)
            P.wait("sync", "V", vXN)
            P.dma("sync", lambda e: e.dma_start(
                out=ag1_in.rearrange("(k p) t -> p k t", p=128),
                in_=XN1[:].rearrange("p (k t) -> p k t", k=NK)))
            dAG1 = P.cnt["DS"]

            P.wait("gpsimd", "DS", dAG1)
            ccAG1 = P.cc(lambda g: g.collective_compute(
                "AllGather", ALU.bypass,
                replica_groups=[[0, 1, 2, 3], [4, 5, 6, 7]],
                ins=[ag1_in], outs=[ag1_out]))

            if core_dbg:
                P.wait("gpsimd", "V", vXN)
                P.dma("gpsimd", lambda e: e.dma_start(
                    out=dbg2.rearrange("(k p) t -> p k t", p=128),
                    in_=XN1[:].rearrange("p (k t) -> p k t", k=NK)))


            WQKV = nc.alloc_sbuf_tensor_at("WQKV", [128, 16 * 1536], BF16, offset=SB_BASE + 0)
            WGT = nc.alloc_sbuf_tensor_at("WGT", [128, 16 * 512], BF16, offset=SB_BASE + 49152)
            WAB = nc.alloc_sbuf_tensor_at("WAB", [128, 16 * 128], BF16, offset=SB_BASE + 65536)
            XNT = [nc.alloc_sbuf_tensor_at("XNT0", [128, 16 * 512], BF16, offset=SB_BASE + 69632),
                   nc.alloc_sbuf_tensor_at("XNT1", [128, 16 * 512], BF16, offset=SB_BASE + 86016)]
            ZR = nc.alloc_sbuf_tensor_at("ZR", [128, 12 * 515], BF16, offset=SB_BASE + 102400)
            SILR = [nc.alloc_sbuf_tensor_at("SIL0", [128, 12 * 512], BF16, offset=SB_BASE + 114784),
                    nc.alloc_sbuf_tensor_at("SIL1", [128, 12 * 512], BF16, offset=SB_BASE + 127072)]
            GSIL = nc.alloc_sbuf_tensor_at("GSIL", [128, 4 * 512], BF16, offset=SB_BASE + 139360)
            VECF = nc.alloc_sbuf_tensor_at("VECF", [128, 2048], F32, offset=SB_BASE + 143456)
            VECB = nc.alloc_sbuf_tensor_at("VECB", [128, 3072], BF16, offset=SB_BASE + 151648)
            CONVA = nc.alloc_sbuf_tensor_at("CONVA", [128, 512], F32, offset=SB_BASE + 157792)
            SCR = nc.alloc_sbuf_tensor_at("SCR", [128, 20 * 512], BF16, offset=SB_BASE + 159840)
            UBCT = nc.alloc_sbuf_tensor_at("UBCT", [128, 512], BF16, offset=SB_BASE + 180320)
            ACB = nc.alloc_sbuf_tensor_at("ACB", [128, 512], BF16, offset=SB_BASE + 181344)
            StF = nc.alloc_sbuf_tensor_at("StF", [128, 512], F32, offset=SB_BASE + 182368)
            StB = nc.alloc_sbuf_tensor_at("StB", [128, 512], BF16, offset=SB_BASE + 184416)
            GATED = nc.alloc_sbuf_tensor_at("GATED", [128, 512], BF16, offset=SB_BASE + 185440)
            ACROW = nc.alloc_sbuf_tensor_at("ACROW", [1, 512], BF16, offset=SB_BASE + 186464)
            RP1 = nc.alloc_sbuf_tensor_at("RP1", [1, 6 * 4 * 256], BF16,
                                          offset=SB_BASE + 191552)
            RP2 = nc.alloc_sbuf_tensor_at("RP2", [1, 2 * 4 * 256], BF16,
                                          offset=SB_BASE + 203840)
            AC_OFF, RINS_OFF, BB_OFF = 512, 1024, 1536
            AINV_OFF, RK_OFF, RQ_OFF, C1_OFF, C2_OFF, C3_OFF = 0, 512, 1024, 1536, 2048, 2560
            iM, iMT = 0, 1
            iP0, iP1, iP2x, iP3 = 2, 3, 4, 5
            iF, iZ0, iZ1, iTINV = 6, 7, 8, 9
            iSCRT, iQAT, iRV, iRBK, iKT2 = 10, 11, 12, 13, 14
            iPP, iGT, iUU, iUT, iFMT = 15, 16, 17, 18, 19

            def scr(i):
                return SCR[:, 512 * i:512 * i + 512]

            STRICT4 = nc.alloc_sbuf_tensor_at("STRICT4", [128, 512], BF16,
                                              offset=SB_BASE + 159840 + 1024 * 4)
            INCL4 = nc.alloc_sbuf_tensor_at("INCL4", [128, 512], BF16,
                                            offset=SB_BASE + 159840 + 1024 * 5)
            IDENT4 = nc.alloc_sbuf_tensor_at("IDENT4", [128, 512], BF16,
                                             offset=SB_BASE + 159840 + 1024 * 9)
            STRICTU4 = nc.alloc_sbuf_tensor_at("STRICTU4", [128, 512], BF16,
                                               offset=SB_BASE + 207936)
            TINYROW = nc.alloc_sbuf_tensor_at("TINYROW", [1, 512], BF16,
                                              offset=SB_BASE + 208960)
            SQB = nc.alloc_sbuf_tensor_at("SQB", [128, 512], BF16,
                                          offset=SB_BASE + 209984)

            def c_p1consts(v):
                v.tensor_single_scalar(WFA[:, 640:768], iota_i[:], 0, ALU.is_lt)
                v.drain()
                for h in range(4):
                    o = 128 * h
                    v.tensor_copy(STRICT4[:, o:o + 128], strictL[:])
                    v.tensor_copy(INCL4[:, o:o + 128], inclU[:])
                    v.tensor_copy(IDENT4[:, o:o + 128], ident[:])
                    v.tensor_copy(STRICTU4[:, o:o + 128], WFA[:, 640:768])
                v.memset(TINYROW[:], 1e-30)
                v.memset(StF[:], 0.0)
                v.memset(StB[:], 0.0)
            P.run("vector", c_p1consts)

            vP1C = P.sig("vector")

            P.wait("gpsimd", "V", vXN)
            P.wait("gpsimd", "DS", dAG1)
            P.dma("gpsimd", lambda e: e.dma_start(
                out=WQKV[:].rearrange("p (k o) -> p k o", k=NK),
                in_=wqkvT.rearrange("(k p) o -> p k o", p=128)))
            P.dma("gpsimd", lambda e: e.dma_start(
                out=WGT[:].rearrange("p (k o) -> p k o", k=NK),
                in_=wgT.rearrange("(k p) o -> p k o", p=128)))
            dW1 = P.dma("gpsimd", lambda e: e.dma_start(
                out=WAB[:].rearrange("p (k o) -> p k o", k=NK),
                in_=wabsp.rearrange("(k p) o -> p k o", p=128)))

            P.wait("vector", "DS", dAG1)

            def v_zinit(v):
                v.memset(ZR[:].rearrange("p (c w) -> p c w", c=12)[:, :, 0:3], 0.0)
                v.memset(VECF[:], 0.0)
                v.memset(VECB[:], 0.0)
            P.run("vector", v_zinit)
            vZI = P.sig("vector")

            with nc.psum_tensor("psQ", [128, 3 * 512], F32) as psQ, \
                 nc.psum_tensor("psAB", [128, 512], F32) as psAB, \
                 nc.psum_tensor("psG", [128, 512], F32) as psG, \
                 nc.psum_tensor("psO1", [128, 512], F32) as psO1, \
                 nc.psum_tensor("psTR", [128, 1536], BF16) as psTR:
                tQ = {}
                vEvQ = {}
                vConv = {}
                sSil = {}
                sGsil = {}
                last_t = 0
                last_v = 0

                def TG(fn):
                    nonlocal last_t
                    P.wait("tensor", "V", last_v)
                    P.run("tensor", fn)
                    last_t = P.sig("tensor")
                    return last_t

                def VG(fn):
                    nonlocal last_v
                    P.wait("vector", "T", last_t)
                    P.run("vector", fn)
                    last_v = P.sig("vector")
                    return last_v

                def rp1s(sec, h, c, n=128):
                    base = h * (6 * 256) + sec * 256 + (c % 2) * 128
                    return RP1[0:1, base:base + n]

                def rp2s(sec, h, c, n=128):
                    base = h * (2 * 256) + sec * 256 + (c % 2) * 128
                    return RP2[0:1, base:base + n]

                def silg(buf, grp, c):
                    cc0 = {"q": 0, "k": 4, "v": 8}[grp]
                    r = SILR[buf][:].rearrange("p (cc w) -> p cc w", cc=12)
                    return r[:, cc0:cc0 + 4, 128 * c:128 * c + 128]

                def sils(buf, grp, h, c):
                    cc0 = {"q": 0, "k": 4, "v": 8}[grp]
                    off = 512 * (cc0 + h) + 128 * c
                    return SILR[buf][:, off:off + 128]

                ps4 = lambda ps: ps[:].rearrange("p (h w) -> p h w", h=4)

                for T in range(NT):
                    buf = T % 2
                    bblk = T // 2
                    toff = 512 * (T % 2)
                    if T == 0:
                        P.wait("sync", "CC", ccAG1)
                    if T >= 2:
                        P.wait("sync", "T", tQ[(T - 2, 12)])
                    dXN = P.dma("sync", lambda e, buf=buf, bblk=bblk, toff=toff: e.dma_start(
                        out=XNT[buf][:].rearrange("p (k t) -> p k t", k=NK),
                        in_=ag1_out[bblk, :, toff:toff + 512].rearrange(
                            "(k p) t -> p k t", p=128)))

                    P.wait("tensor", "DS", dXN)
                    if T == 0:
                        P.wait("tensor", "DG", dW1)
                        P.wait("tensor", "V", vP1C)
                    for oc in range(12):
                        bank = oc % 3
                        if (T, oc - 3) in vEvQ:
                            P.wait("tensor", "V", vEvQ[(T, oc - 3)])
                        elif T > 0 and oc < 3:
                            P.wait("tensor", "V", vEvQ[(T - 1, 9 + oc)])

                        def mm_q(t, buf=buf, oc=oc, bank=bank):
                            for k in range(16):
                                t.matmul(psQ[:, 512 * bank:512 * bank + 512],
                                         WQKV[:, 1536 * k + 128 * oc:1536 * k + 128 * oc + 128],
                                         XNT[buf][:, 512 * k:512 * k + 512],
                                         start=(k == 0), stop=(k == 15))
                        P.run("tensor", mm_q)
                        tQ[(T, oc)] = P.sig("tensor")
                        zslc = ZR[:, 515 * oc + 3:515 * oc + 515]
                        if oc % 2 == 0:
                            P.wait("vector", "T", tQ[(T, oc)])
                            if T > 0:
                                P.wait("vector", "V", vConv[(T - 1, oc)])

                            def v_ev(v, zslc=zslc, bank=bank):
                                v.tensor_copy(zslc, psQ[:, 512 * bank:512 * bank + 512])
                            P.run("vector", v_ev)
                            vEvQ[(T, oc)] = P.sig("vector")
                        else:
                            P.wait("scalar", "T", tQ[(T, oc)])
                            if T == 0:
                                P.wait("scalar", "V", vZI)
                            if T > 0:
                                P.wait("scalar", "V", vConv[(T - 1, oc)])

                            def s_ev(s, zslc=zslc, bank=bank):
                                s.copy(zslc, psQ[:, 512 * bank:512 * bank + 512])
                            P.run("scalar", s_ev)
                            sev = P.sig("scalar")
                            P.wait("vector", "S", sev)
                            vEvQ[(T, oc)] = P.sig("vector")

                    def mm_ab(t, buf=buf):
                        for k in range(16):
                            t.matmul(psAB[:], WAB[:, 128 * k:128 * k + 128],
                                     XNT[buf][:, 512 * k:512 * k + 512],
                                     start=(k == 0), stop=(k == 15))
                    if T > 0:
                        P.wait("tensor", "V", last_v)
                    P.run("tensor", mm_ab)
                    tQ[(T, 12)] = P.sig("tensor")

                    P.wait("scalar", "T", tQ[(T, 12)])
                    if T == 0:
                        P.wait("scalar", "V", vZI)

                    def s_sig(s):
                        s.activation(VECF[:, 0:512], psAB[:], AF.Sigmoid,
                                     bias=babcol[:])
                    P.run("scalar", s_sig)
                    sAB = P.sig("scalar")

                    P.wait("vector", "S", sAB)

                    def v_acum(v):
                        for j in range(4):
                            v.tensor_tensor_scan(
                                VECF[:, AC_OFF + 128 * j:AC_OFF + 128 * j + 128],
                                VECF[:, 128 * j:128 * j + 128],
                                zeros128[:], 1.0, ALU.mult, ALU.add)
                        v.drain()
                        v.tensor_scalar(VECF[:, AC_OFF:AC_OFF + 512],
                                        VECF[:, AC_OFF:AC_OFF + 512], 1e-30, None,
                                        ALU.max)
                        v.drain()
                        v.reciprocal(VECB[:, AINV_OFF:AINV_OFF + 512],
                                     VECF[:, AC_OFF:AC_OFF + 512])
                    P.run("vector", v_acum)
                    vACm = P.sig("vector")

                    P.wait("sync", "V", vACm)
                    for h_ in range(4):
                        dBB = P.dma("sync", lambda e, h_=h_: e.dma_start(
                            out=VECF[32 * h_:32 * h_ + 1, BB_OFF:BB_OFF + 512],
                            in_=VECF[32 * h_ + 16:32 * h_ + 17, 0:512]))

                    for cc in range(12):
                        P.wait("vector", "V", vEvQ[(T, cc)])
                        if T == 0 and cc == 0:
                            P.wait("vector", "DS", dSmall)

                        def v_conv(v, cc=cc):
                            zb = 515 * cc
                            v.tensor_scalar(CONVA[:], ZR[:, zb:zb + 512],
                                            cwc[:, 4 * cc:4 * cc + 1], None, ALU.mult)
                            v.drain()
                            for tau in range(1, 4):
                                v.scalar_tensor_tensor(
                                    CONVA[:], ZR[:, zb + tau:zb + tau + 512],
                                    cwc[:, 4 * cc + tau:4 * cc + tau + 1],
                                    CONVA[:], ALU.mult, ALU.add)
                                v.drain()
                            v.tensor_copy(ZR[:, zb:zb + 3], ZR[:, zb + 512:zb + 515])
                        P.run("vector", v_conv)
                        vConv[(T, cc)] = P.sig("vector")

                        P.wait("scalar", "V", vConv[(T, cc)])

                        def s_sil(s, cc=cc, buf=buf):
                            s.activation(SILR[buf][:, 512 * cc:512 * cc + 512],
                                         CONVA[:], AF.Silu)
                        P.run("scalar", s_sil)
                        sSil[(T, cc)] = P.sig("scalar")

                    for j in range(4):
                        if j == 0:
                            P.wait("tensor", "S", sSil[(T, 11)])
                            if T > 0:
                                P.wait("tensor", "V", last_v)
                        else:
                            P.wait("tensor", "S", sGsil[(T, j - 1)])

                        def mm_g(t, buf=buf, j=j):
                            for k in range(16):
                                t.matmul(psG[:],
                                         XNT[buf][:, 512 * k + 128 * j:512 * k + 128 * j + 128],
                                         WGT[:, 512 * k:512 * k + 512],
                                         start=(k == 0), stop=(k == 15))
                        P.run("tensor", mm_g)
                        tg = P.sig("tensor")
                        P.wait("scalar", "T", tg)

                        def s_gsil(s, j=j):
                            s.activation(GSIL[:, 512 * j:512 * j + 512], psG[:], AF.Silu)
                        P.run("scalar", s_gsil)
                        sGsil[(T, j)] = P.sig("scalar")
                    last_t = tg

                    for grp, off in (("k", RK_OFF), ("q", RQ_OFF)):
                        P.wait("tensor", "S", sGsil[(T, 3)])
                        if grp == "q":
                            P.wait("tensor", "V", last_v)

                        def mm_zf(t):
                            t.matmul(psAB[:], ones_r_bf[:], TINYROW[:],
                                     start=True, stop=False, skip_group_check=True)
                        P.run("tensor", mm_zf)
                        tzf = P.sig("tensor")
                        for h in range(4):
                            if h == 0:
                                P.wait("scalar", "V", vConv[(T, 11)])
                                P.wait("scalar", "T", tzf)

                            def s_sq(s, h=h, grp=grp, buf=buf):
                                cc = (4 + h) if grp == "k" else h
                                s.activation(SQB[:], SILR[buf][:, 512 * cc:512 * cc + 512],
                                             AF.Square)
                            P.run("scalar", s_sq)
                            ssq_h = P.sig("scalar")
                            P.wait("tensor", "S", ssq_h)

                            def mm_ssq(t, h=h):
                                t.matmul(psAB[32 * h:32 * h + 1, :], ones_c_bf[:],
                                         SQB[:], start=False, stop=(h == 3),
                                         tile_position=(0, 32 * h),
                                         skip_group_check=True)
                            P.run("tensor", mm_ssq)
                        last_t = P.sig("tensor")

                        P.wait("vector", "T", last_t)

                        def v_rins(v):
                            v.reciprocal(VECF[:, RINS_OFF:RINS_OFF + 512], psAB[:])
                        P.run("vector", v_rins)
                        vri = P.sig("vector")
                        P.wait("scalar", "V", vri)

                        def s_rsq(s):
                            s.activation(VECF[:, RINS_OFF:RINS_OFF + 512],
                                         VECF[:, RINS_OFF:RINS_OFF + 512], AF.Sqrt)
                        P.run("scalar", s_rsq)
                        srs = P.sig("scalar")
                        P.wait("vector", "S", srs)

                        def v_rmin(v, off=off):
                            v.tensor_scalar(VECB[:, off:off + 512],
                                            VECF[:, RINS_OFF:RINS_OFF + 512],
                                            1e12, None, ALU.min)
                        P.run("vector", v_rmin)
                        last_v = P.sig("vector")

                    P.wait("vector", "DS", dBB)

                    def v_cvec(v):
                        v.tensor_mul(VECF[:, 0:512], VECF[:, BB_OFF:BB_OFF + 512],
                                     VECF[:, AC_OFF:AC_OFF + 512])
                        v.drain()
                        v.tensor_mul(VECB[:, C1_OFF:C1_OFF + 512], VECF[:, 0:512],
                                     VECB[:, RK_OFF:RK_OFF + 512])
                        v.tensor_mul(VECB[:, C2_OFF:C2_OFF + 512],
                                     VECB[:, RK_OFF:RK_OFF + 512],
                                     VECB[:, AINV_OFF:AINV_OFF + 512])
                        v.tensor_mul(VECB[:, C3_OFF:C3_OFF + 512],
                                     VECF[:, AC_OFF:AC_OFF + 512],
                                     VECB[:, RQ_OFF:RQ_OFF + 512])
                    P.run("vector", v_cvec)
                    vCV = P.sig("vector")
                    last_v = vCV

                    dRP = {}
                    for half in range(2):
                        P.wait("sync", "V", vCV)
                        if T > 0 or half > 0:
                            P.wait("sync", "T", last_t)
                        for h_ in range(4):
                            P.dma("sync", lambda e, half=half, h_=h_: e.dma_start(
                                out=RP1[0:1, h_ * 1536:h_ * 1536 + 1536].rearrange(
                                    "o (s c) -> o s c", s=6),
                                in_=VECB[32 * h_:32 * h_ + 1, :].rearrange(
                                    "o (s w) -> o s w", s=6)[:, :, 256 * half:256 * half + 256]))
                            P.dma("gpsimd", lambda e, half=half, h_=h_: e.dma_start(
                                out=RP2[0:1, h_ * 512:h_ * 512 + 512].rearrange(
                                    "o (s c) -> o s c", s=2),
                                in_=VECF[32 * h_:32 * h_ + 1, :].rearrange(
                                    "o (s w) -> o s w", s=4)[:, 1:4:2,
                                                            256 * half:256 * half + 256]))
                        dRP[half] = P.cnt["DG"]

                        for c in (2 * half, 2 * half + 1):
                            P.wait("tensor", "DS", P.cnt["DS"])
                            P.wait("tensor", "DG", dRP[half])
                            def t1(t, buf=buf, c=c):
                                for h in range(4):
                                    ks = sils(buf, "k", h, c)
                                    t.matmul(psO1[:, 128 * h:128 * h + 128], ks, ks,
                                             start=True, stop=True)
                                    t.matmul(psG[:, 128 * h:128 * h + 128],
                                             rp1s(3, h, c), rp1s(4, h, c),
                                             start=True, stop=True)
                            TG(t1)

                            def v2a(v):
                                v.tensor_mul(scr(iM), psO1[:], STRICT4[:])
                                v.drain()
                                v.tensor_mul(scr(iM), scr(iM), psG[:])
                            VG(v2a)

                            def t1b(t, c=c):
                                for h in range(4):
                                    t.matmul(psG[:, 128 * h:128 * h + 128],
                                             rp1s(4, h, c), rp1s(3, h, c),
                                             start=True, stop=True)
                            TG(t1b)

                            def v2b(v):
                                v.tensor_mul(scr(iMT), psO1[:], STRICTU4[:])
                                v.drain()
                                v.tensor_mul(scr(iMT), scr(iMT), psG[:])
                                v.drain()
                                v.scalar_tensor_tensor(scr(iZ0), scr(iMT), -1.0,
                                                       IDENT4[:], ALU.mult, ALU.add)
                            VG(v2b)

                            def t3(t, buf=buf, c=c):
                                for h in range(4):
                                    t.matmul(psO1[:, 128 * h:128 * h + 128],
                                             sils(buf, "k", h, c), sils(buf, "q", h, c),
                                             start=True, stop=True)
                                    t.matmul(psG[:, 128 * h:128 * h + 128],
                                             rp1s(4, h, c), rp1s(5, h, c),
                                             start=True, stop=True)
                            TG(t3)

                            def v4a(v):
                                v.tensor_mul(scr(iSCRT), psO1[:], INCL4[:])
                                v.drain()
                                v.tensor_mul(scr(iSCRT), scr(iSCRT), psG[:])
                            VG(v4a)

                            def t3b(t, c=c):
                                for h in range(4):
                                    t.matmul(psG[:, 128 * h:128 * h + 128],
                                             ones_r_bf[:], rp1s(5, h, c),
                                             start=True, stop=True)
                            TG(t3b)

                            def v4b(v, buf=buf, c=c):
                                v.tensor_mul(scr(iQAT).rearrange("p (h w) -> p h w", h=4),
                                             silg(buf, "q", c), ps4(psG))
                            VG(v4b)

                            def t5(t, c=c):
                                for h in range(4):
                                    t.matmul(psG[:, 128 * h:128 * h + 128],
                                             ones_r_bf[:], rp2s(1, h, c),
                                             start=True, stop=True)
                                    t.matmul(psO1[:, 128 * h:128 * h + 128],
                                             ones_r_bf[:], rp1s(3, h, c),
                                             start=True, stop=True)
                            TG(t5)

                            def v6a(v, buf=buf, c=c):
                                v.tensor_mul(scr(iFMT).rearrange("p (h w) -> p h w", h=4),
                                             silg(buf, "v", c), ps4(psG))
                                v.tensor_mul(scr(iPP).rearrange("p (h w) -> p h w", h=4),
                                             silg(buf, "k", c), ps4(psO1))
                            VG(v6a)

                            def t5b(t, c=c):
                                for h in range(4):
                                    t.matmul(psG[:, 128 * h:128 * h + 128],
                                             ones_r_bf[:], rp1s(1, h, c),
                                             start=True, stop=True)
                                    acb = h * (2 * 256) + 0 * 256 + (c % 2) * 128 + 127
                                    t.matmul(psAB[0:1, 128 * h:128 * h + 128],
                                             RP2[0:1, acb:acb + 1],
                                             ones_r_bf[:], start=True, stop=True)
                            TG(t5b)

                            def v6b(v, buf=buf, c=c):
                                v.tensor_mul(scr(iGT).rearrange("p (h w) -> p h w", h=4),
                                             silg(buf, "k", c), ps4(psG))
                                v.tensor_copy(ACROW[0:1, :], psAB[0:1, :])
                            VG(v6b)

                            def t7(t, c=c):
                                for h in range(4):
                                    t.transpose(psTR[:, 128 * h:128 * h + 128],
                                                scr(iFMT)[:, 128 * h:128 * h + 128],
                                                ident[:])
                                    t.transpose(psTR[:, 512 + 128 * h:512 + 128 * h + 128],
                                                scr(iPP)[:, 128 * h:128 * h + 128],
                                                ident[:])
                                    t.transpose(psTR[:, 1024 + 128 * h:1024 + 128 * h + 128],
                                                scr(iGT)[:, 128 * h:128 * h + 128],
                                                ident[:])
                                    t.matmul(psO1[:, 128 * h:128 * h + 128],
                                             rp1s(0, h, c),
                                             ACROW[0:1, 128 * h:128 * h + 128],
                                             start=True, stop=True)
                                    t.matmul(psG[:, 128 * h:128 * h + 128],
                                             ones_r_bf[:],
                                             ACROW[0:1, 128 * h:128 * h + 128],
                                             start=True, stop=True)
                            TG(t7)

                            def v8(v):
                                v.tensor_copy(scr(iRV), psTR[:, 0:512])
                                v.tensor_copy(scr(iRBK), psTR[:, 512:1024])
                                v.tensor_copy(scr(iKT2), psTR[:, 1024:1536])
                                v.tensor_copy(UBCT[:], psO1[:])
                                v.tensor_copy(ACB[:], psG[:])
                            VG(v8)

                            levels = [
                                (iMT, iM, iP0, iP1, True),
                                (iP1, iP0, iM, iMT, True),
                                (iMT, iM, iP0, iP1, True),
                                (iP1, iP0, iM, iMT, True),
                                (iMT, iM, iP0, iP1, True),
                                (iP1, iP0, iM, None, False),
                            ]
                            zc, zn = iZ0, iZ1
                            for (lt, ln_, dn, dt, both) in levels:
                                def t9(t, lt=lt, ln_=ln_, both=both):
                                    for h in range(4):
                                        sl = slice(128 * h, 128 * h + 128)
                                        t.matmul(psQ[:, 0:512][:, sl], scr(lt)[:, sl],
                                                 scr(ln_)[:, sl], start=True, stop=True)
                                        if both:
                                            t.matmul(psQ[:, 512:1024][:, sl],
                                                     scr(ln_)[:, sl], scr(lt)[:, sl],
                                                     start=True, stop=True)
                                TG(t9)

                                def v9(v, dn=dn, dt=dt, both=both):
                                    v.tensor_copy(scr(dn), psQ[:, 0:512])
                                    if both:
                                        v.tensor_copy(scr(dt), psQ[:, 512:1024])
                                    v.drain()
                                    v.tensor_add(scr(iF), scr(dn), IDENT4[:])
                                VG(v9)

                                def t10(t, zc=zc):
                                    for h in range(4):
                                        sl = slice(128 * h, 128 * h + 128)
                                        t.matmul(psQ[:, 1024:1536][:, sl],
                                                 scr(iF)[:, sl], scr(zc)[:, sl],
                                                 start=True, stop=True)
                                TG(t10)

                                def v10(v, zn=zn):
                                    v.tensor_copy(scr(zn), psQ[:, 1024:1536])
                                VG(v10)
                                zc, zn = zn, zc
                            def t12(t, zc=zc):
                                for h in range(4):
                                    sl = slice(128 * h, 128 * h + 128)
                                    t.matmul(psQ[:, 0:512][:, sl], scr(zc)[:, sl],
                                             scr(iRV)[:, sl], start=True, stop=True)
                                    t.matmul(psQ[:, 512:1024][:, sl],
                                             scr(iRBK)[:, sl], scr(zc)[:, sl],
                                             start=True, stop=True)
                            TG(t12)

                            def v12(v):
                                v.tensor_copy(scr(iPP), psQ[:, 0:512])
                                v.tensor_copy(scr(iGT), psQ[:, 512:1024])
                            VG(v12)

                            def t13(t):
                                for h in range(4):
                                    sl = slice(128 * h, 128 * h + 128)
                                    t.matmul(psO1[:, sl], scr(iGT)[:, sl], StB[:, sl],
                                             start=True, stop=True)
                            TG(t13)

                            def v13(v):
                                v.tensor_tensor(scr(iUU), scr(iPP), psO1[:], ALU.subtract)
                                v.drain()
                                v.tensor_mul(scr(iUT), scr(iUU), UBCT[:])
                            VG(v13)

                            def t14(t):
                                for h in range(4):
                                    sl = slice(128 * h, 128 * h + 128)
                                    t.matmul(psO1[:, sl], scr(iSCRT)[:, sl],
                                             scr(iUU)[:, sl], start=True, stop=False)
                                    t.matmul(psO1[:, sl], scr(iQAT)[:, sl],
                                             StB[:, sl], start=False, stop=True)
                                    t.matmul(psG[:, sl], scr(iKT2)[:, sl],
                                             scr(iUT)[:, sl], start=True, stop=True)
                            TG(t14)

                            def v14(v, c=c):
                                v.tensor_mul(CONVA[:], StF[:], ACB[:])
                                v.drain()
                                v.tensor_add(StF[:], CONVA[:], psG[:])
                                v.drain()
                                v.tensor_copy(StB[:], StF[:])
                                v.tensor_mul(GATED[:], GSIL[:, 512 * c:512 * c + 512],
                                             psO1[:])
                            VG(v14)

                            P.wait("sync", "V", last_v)
                            P.dma("sync", lambda e, T=T, c=c: e.dma_start(
                                out=ag2_in[512 * T + 128 * c:512 * T + 128 * c + 128, :],
                                in_=GATED[:]))
                dAG2w = P.cnt["DS"]

            P.wait("gpsimd", "DS", dAG2w)
            ccAG2 = P.cc(lambda g: g.collective_compute(
                "AllGather", ALU.bypass,
                replica_groups=[[0, 1, 2, 3], [4, 5, 6, 7]],
                ins=[ag2_in], outs=[ag2_out]))

            X2 = nc.alloc_sbuf_tensor_at("X2", [128, NK * TOK], F32, offset=SB_BASE + 0)
            XB2 = nc.alloc_sbuf_tensor_at("XB2", [128, NK * TOK], BF16, offset=SB_BASE + 65536)
            OUTF = nc.alloc_sbuf_tensor_at("OUTF", [128, NK * TOK], BF16, offset=SB_BASE + 98304)
            WOT = nc.alloc_sbuf_tensor_at("WOT", [128, 16 * 1024], BF16, offset=SB_BASE + 131072)
            WGAM2 = nc.alloc_sbuf_tensor_at("WGAM2", [128, NK * D], BF16, offset=SB_BASE + 98304)

            P.wait("sync", "CC", ccAG2)
            dXM2 = P.dma("sync", lambda e: e.dma_start(
                out=X2[:].rearrange("p (k t) -> p k t", k=NK),
                in_=xmid_scr.rearrange("(k p) t -> p k t", p=128)))
            P.wait("scalar", "CC", ccAG2)
            _offreg = []
            dOF = 0
            for j in range(4):
                for m_ in range(4):
                    e_ = 4 * j + m_

                    def s_tr(e, j=j, m_=m_, e_=e_):
                        if not _offreg:
                            _offreg.append(e.alloc_register("coreoff_r"))
                            _offreg.append(e.alloc_register("coreoff_r2"))
                            e.reg_load(_offreg[0], coreoff_sb[0:1, 0:1])
                        e.reg_add(_offreg[1], _offreg[0],
                                  j * (NSEQ * 512) + 128 * m_)
                        src = bass.AP(ag2_out.tensor, _offreg[1],
                                      [[512, 1024], [1, 128]])
                        return e.dma_start_transpose(
                            OUTF[:, TOK * e_:TOK * e_ + TOK], src)
                    dOF = P.dma("scalar", s_tr)
            dWO = {}
            dWO[0] = P.dma("gpsimd", lambda e: e.dma_start(
                out=WOT[:].rearrange("p (k o) -> p k o", k=NK),
                in_=woT.rearrange("(k p) o -> p k o", p=128)[:, :, 0:1024]))

            with nc.psum_tensor("psW", [128, 3 * 512], F32) as psW:
                vW = 0
                v_sigs = []
                it = 0
                for half in range(2):
                    if half == 1:
                        P.wait("gpsimd", "T", last_t)
                        dWO[1] = P.dma("gpsimd", lambda e: e.dma_start(
                            out=WOT[:].rearrange("p (k o) -> p k o", k=NK),
                            in_=woT.rearrange("(k p) o -> p k o", p=128)[:, :, 1024:2048]))
                    for oc8 in range(8):
                        oc = 8 * half + oc8
                        for t2 in range(2):
                            bank = it % 3
                            if half == 0 and oc8 == 0 and t2 == 0:
                                P.wait("tensor", "DA", dOF)
                                P.wait("tensor", "DG", dWO[0])
                                P.wait("tensor", "DS", dXM2)
                            if half == 1 and oc8 == 0 and t2 == 0:
                                P.wait("tensor", "DG", dWO[1])
                            if it >= 3:
                                P.wait("tensor", "V", v_sigs[it - 3])

                            def mm_w(t, oc8=oc8, t2=t2, bank=bank):
                                for k in range(16):
                                    t.matmul(psW[:, 512 * bank:512 * bank + 512],
                                             WOT[:, 1024 * k + 128 * oc8:1024 * k + 128 * oc8 + 128],
                                             OUTF[:, TOK * k + 512 * t2:TOK * k + 512 * t2 + 512],
                                             start=(k == 0), stop=(k == 15))
                            P.run("tensor", mm_w)
                            last_t = P.sig("tensor")
                            P.wait("vector", "T", last_t)

                            def v_wev(v, oc=oc, t2=t2, bank=bank):
                                xs = X2[:, TOK * oc + 512 * t2:TOK * oc + 512 * t2 + 512]
                                v.tensor_add(xs, xs, psW[:, 512 * bank:512 * bank + 512])
                            P.run("vector", v_wev)
                            vW = P.sig("vector")
                            v_sigs.append(vW)
                            it += 1
            vATT = vW

            P.wait("gpsimd", "V", vATT)
            dWG2 = P.dma("gpsimd", lambda e: e.dma_start(
                out=WGAM2[:].rearrange("p (k o) -> p k o", k=NK),
                in_=wgam_scr.rearrange("(k p) o -> p k o", p=128)))

            vxb3 = {}
            for k in range(16):
                if k == 0:
                    P.wait("vector", "V", vATT)

                def v_xb3(v, k=k):
                    v.tensor_copy(XB2[:, TOK * k:TOK * k + TOK], X2[:, TOK * k:TOK * k + TOK])
                P.run("vector", v_xb3)
                vxb3[k] = P.sig("vector")

            sq_waits3 = {k: vATT for k in range(16)}
            vBC3 = ln_stats("3", "V", sq_waits3,
                            lambda k, j: XB2[:, TOK * k + 512 * j:TOK * k + 512 * j + 512],
                            inner_ln=True, sum_wait=("V", vxb3), guard_v=vATT,
                            xsrc=X2)
            P.wait("tensor", "DG", dWG2)
            P.wait("tensor", "DS", vBC3)
            vFFN2 = ffn("2", wgam=WGAM2, xb=XB2, xdst=X2)

            P.wait("sync", "V", vFFN2)
            for k in range(16):
                P.dma("sync", lambda e, k=k: e.dma_start(
                    out=y_fm[128 * k:128 * k + 128, :], in_=X2[:, TOK * k:TOK * k + TOK]))
            dOUT = P.cnt["DS"]
            P.wait("sync", "DS", dOUT)
            P.wait("gpsimd", "DS", dOUT)
            P.wait("vector", "DS", dOUT)
            P._complete = True

        except _StopRecord:
            pass
        with nc.allow_low_precision("bf16 intermediates; tolerance calibrated "
                                    "for bf16 matmuls with fp32 accumulation"):
            if not getattr(P, "_complete", False):
                _emit_tail(nc, P, stack, X, y_fm, TOK)
            else:
                blk = stack.enter_context(nc.Block())
                P.emit(blk)
    return nc


def _emit_tail(nc, P, stack, X, y_fm, TOK):
    P.run("vector", lambda v: v.memset(X[:, 0:TOK], 0.0))
    vZ = P.sig("vector")
    P.wait("sync", "V", vZ)
    for k in range(16):
        P.dma("sync", lambda e, k=k: e.dma_start(
            out=y_fm[128 * k:128 * k + 128, :], in_=X[:, 0:TOK]))
    dOUT = P.cnt["DS"]
    P.wait("sync", "DS", dOUT)
    P.wait("gpsimd", "DS", dOUT)

    blk = stack.enter_context(nc.Block())
    P.emit(blk)

    return nc


def _prep_inputs(L0_attn_x, L0_attn_W_qkv, L0_attn_conv_w, L0_attn_W_ab,
                 L0_attn_b_ab, L0_attn_W_g, L0_attn_W_o, L0_ffn_weight,
                 L0_ffn_bias, L0_ffn_ln_gamma, L0_ffn_ln_beta):
    asc = np.ascontiguousarray
    f32 = np.float32
    in_maps = []
    wfT = asc(np.asarray(L0_ffn_weight, f32).T)
    woT = asc(np.asarray(L0_attn_W_o, f32).T)
    gcols = asc(np.asarray(L0_ffn_ln_gamma, f32).reshape(16, 128).T)
    bcols = asc(np.asarray(L0_ffn_ln_beta, f32).reshape(16, 128).T)
    fbcols = asc(np.asarray(L0_ffn_bias, f32).reshape(16, 128).T)
    Wqkv = np.asarray(L0_attn_W_qkv, f32)
    Wab = np.asarray(L0_attn_W_ab, f32)
    bab = np.asarray(L0_attn_b_ab, f32)
    Wg = np.asarray(L0_attn_W_g, f32)
    cw = np.asarray(L0_attn_conv_w, f32)
    x = np.asarray(L0_attn_x, f32)
    for c in range(8):
        g, r = divmod(c, 4)
        h0 = 4 * r
        rows_q = slice(128 * h0, 128 * h0 + 512)
        rows_k = slice(2048 + 128 * h0, 2048 + 128 * h0 + 512)
        rows_v = slice(4096 + 128 * h0, 4096 + 128 * h0 + 512)
        wqkv_rows = np.concatenate([Wqkv[rows_q], Wqkv[rows_k], Wqkv[rows_v]], 0)
        wqkvT = asc(wqkv_rows.T)
        wabsp = np.zeros((2048, 128), f32)
        babcol = np.zeros((128, 1), f32)
        for h in range(4):
            wabsp[:, 32 * h] = Wab[h0 + h]
            wabsp[:, 32 * h + 16] = Wab[16 + h0 + h]
            babcol[32 * h, 0] = bab[h0 + h]
            babcol[32 * h + 16, 0] = bab[16 + h0 + h]
        wgT = asc(Wg[128 * h0:128 * h0 + 512].T)
        ch_rows = np.concatenate(
            [cw[rows_q], cw[rows_k], cw[rows_v]], 0)
        cw48 = asc(ch_rows.reshape(12, 128, 4).transpose(1, 0, 2).reshape(128, 48))
        xT = asc(x[g, 1024 * r:1024 * r + 1024].T)
        coreoff = np.array([[1024 * r * 512]], np.int32)
        in_maps.append(dict(
            xT=xT, wfT=wfT, wqkvT=wqkvT, wabsp=wabsp, wgT=wgT, woT=woT,
            cw48=cw48, gcols_d=gcols, bcols_d=bcols, fbcols_d=fbcols,
            babcol_d=babcol, coreoff_d=coreoff))
    return in_maps


_nc_cache = {}


def kernel(**inputs):
    in_maps = _prep_inputs(**inputs)
    key = "dbg" if DEBUG else "plain"
    if key not in _nc_cache:
        _nc_cache[key] = build_program(core_dbg=DEBUG)
    nc = _nc_cache[key]
    res = run_bass_kernel_spmd(nc, in_maps, core_ids=list(range(8)))
    out = np.empty((2, 4096, 2048), np.float32)
    for c in range(8):
        g, r = divmod(c, 4)
        out[g, 1024 * r:1024 * r + 1024, :] = res.results[c]["y_fm"].T
    if DEBUG:
        kernel.debug_results = res.results
    return out


# version 03
# speedup vs baseline: 1.0284x; 1.0284x over previous
import os
os.environ.setdefault("TRN_ALLOW_ICI_ALL_TO_ALL", "1")
import numpy as np
import ml_dtypes
import concourse.bass as bass
import concourse.mybir as mybir
from concourse.bass_isa import ReduceOp
from concourse.bass_utils import run_bass_kernel_spmd
from contextlib import ExitStack

F32 = mybir.dt.float32
BF16 = mybir.dt.bfloat16
I32 = mybir.dt.int32
AF = mybir.ActivationFunctionType
ALU = mybir.AluOpType
AX = mybir.AxisListType
BF = ml_dtypes.bfloat16

D = 2048
NK = 16
TOK = 1024
NSEQ = 4096
NT = 8
LN_EPS = 1e-5

DEBUG = False
BUILD_STOP = 99


class _StopRecord(Exception):
    pass


class Prog:
    ENGINES = ("sync", "scalar", "vector", "tensor", "gpsimd")
    EPROG = {"tensor": "T", "vector": "V", "scalar": "S", "gpsimd": "G"}
    EDMA = {"sync": "DS", "scalar": "DA", "gpsimd": "DG", "vector": "DV"}

    def __init__(self, nc, stack):
        self.nc = nc
        self.items = {e: [] for e in self.ENGINES}
        self.sem = {}
        self.cnt = {}
        for nm in ("T", "V", "S", "G", "DS", "DA", "DG", "DV", "CC"):
            self.sem[nm] = stack.enter_context(nc.semaphore("sem_" + nm))
            self.cnt[nm] = 0

    def wait(self, eng, nm, val):
        if val > 0:
            self.items[eng].append(("w", nm, val))

    def run(self, eng, fn):
        self.items[eng].append(("op", fn))

    def sig(self, eng, k=1):
        nm = self.EPROG[eng]
        self.cnt[nm] += k
        self.items[eng].append(("sig", nm, k))
        return self.cnt[nm]

    def dma(self, eng, fn):
        nm = self.EDMA[eng]
        self.cnt[nm] += 16
        self.items[eng].append(("dma", fn, nm))
        return self.cnt[nm]

    def cc(self, fn):
        self.cnt["CC"] += 1
        self.items["gpsimd"].append(("cc", fn))
        return self.cnt["CC"]

    def emit(self, blk):
        for ename in self.ENGINES:
            items = self.items[ename]

            def body(eng, items=items):
                for it in items:
                    if it[0] == "w":
                        eng.wait_ge(self.sem[it[1]], it[2])
                    elif it[0] == "op":
                        it[1](eng)
                    elif it[0] == "sig":
                        eng.drain().then_inc(self.sem[it[1]], it[2])
                    elif it[0] == "dma":
                        it[1](eng).then_inc(self.sem[it[2]], 16)
                    elif it[0] == "cc":
                        it[1](eng).then_inc(self.sem["CC"], 1)

            getattr(blk, ename)(body)


SB_BASE = 16512


class Mem:

    def __init__(self, nc, limit=212800):
        self.nc = nc
        self.limit = limit
        self.names = {}

    def at(self, name, off, shape, dtype):
        esz = 2 if dtype in (BF16,) else 4
        nbytes = int(np.prod(shape[1:])) * esz if len(shape) > 1 else esz
        assert off + nbytes <= self.limit, (name, off, nbytes)
        assert off % 32 == 0, (name, off)
        self.names[name] = (off, nbytes)
        return self.nc.alloc_sbuf_tensor_at(name, list(shape), dtype,
                                            offset=SB_BASE + off)


def build_program(core_dbg=False, stop=None, race_check=True):
    if stop is None:
        stop = BUILD_STOP
    nc = bass.Bass("TRN2", target_bir_lowering=False,
                   detect_race_conditions=race_check)

    xT = nc.dram_tensor("xT", [D, TOK], F32, kind="ExternalInput").ap()
    wfT = nc.dram_tensor("wfT", [D, D], F32, kind="ExternalInput").ap()
    wqkvT = nc.dram_tensor("wqkvT", [D, 1536], F32, kind="ExternalInput").ap()
    wabsp = nc.dram_tensor("wabsp", [D, 128], F32, kind="ExternalInput").ap()
    wgT = nc.dram_tensor("wgT", [D, 512], F32, kind="ExternalInput").ap()
    woT = nc.dram_tensor("woT", [D, D], F32, kind="ExternalInput").ap()
    cw48 = nc.dram_tensor("cw48", [128, 48], F32, kind="ExternalInput").ap()
    gcols_d = nc.dram_tensor("gcols_d", [128, 16], F32, kind="ExternalInput").ap()
    bcols_d = nc.dram_tensor("bcols_d", [128, 16], F32, kind="ExternalInput").ap()
    fbcols_d = nc.dram_tensor("fbcols_d", [128, 16], F32, kind="ExternalInput").ap()
    babcol_d = nc.dram_tensor("babcol_d", [128, 1], F32, kind="ExternalInput").ap()
    coreoff_d = nc.dram_tensor("coreoff_d", [1, 1], I32, kind="ExternalInput").ap()

    y_fm = nc.dram_tensor("y_fm", [D, TOK], F32, kind="ExternalOutput").ap()
    if core_dbg:
        dbg = nc.dram_tensor("dbg", [D, TOK], F32, kind="ExternalOutput").ap()
        dbg2 = nc.dram_tensor("dbg2", [D, TOK], F32, kind="ExternalOutput").ap()
        dbg_xb = nc.dram_tensor("dbg_xb", [D, TOK], F32, kind="ExternalOutput").ap()
        dbg_wg = nc.dram_tensor("dbg_wg", [D, D], F32, kind="ExternalOutput").ap()
        dbg_sb = nc.dram_tensor("dbg_sb", [128, 2 * TOK + 32], F32, kind="ExternalOutput").ap()

    ag1_in = nc.dram_tensor("ag1_in", [D, TOK], BF16).ap()
    ag1_out = nc.dram_tensor("ag1_out", [4, D, TOK], BF16).ap()
    ag2_in = nc.dram_tensor("ag2_in", [NSEQ, 512], BF16).ap()
    ag2_out = nc.dram_tensor("ag2_out", [4, NSEQ, 512], BF16).ap()
    xmid_scr = nc.dram_tensor("xmid_scr", [D, TOK], F32).ap()
    bc_scr = nc.dram_tensor("bc_scr", [1, 2 * TOK + 32], F32).ap()
    wgam_scr = nc.dram_tensor("wgam_scr", [D, D], BF16).ap()

    with ExitStack() as stack:
        P = Prog(nc, stack)
        M = Mem(nc)
        _tailref = {}

        def _ck(n):
            if stop <= n:
                raise _StopRecord()

        CO = 188416
        def cst(name, shape, dtype):
            nonlocal CO
            h = M.at(name, CO, shape, dtype)
            esz = 2 if dtype == BF16 else 4
            nb = int(np.prod(shape[1:])) * esz
            CO += (nb + 31) & ~31
            return h
        ident = cst("ident", [128, 128], BF16)
        strictL = cst("strictL", [128, 128], BF16)
        inclU = cst("inclU", [128, 128], BF16)
        ones_r_bf = cst("ones_r_bf", [1, 128], BF16)
        ones_r_f = cst("ones_r_f", [1, 128], F32)
        ones_c_bf = cst("ones_c_bf", [128, 1], BF16)
        ones_c_f = cst("ones_c_f", [128, 1], F32)
        iota_i = cst("iota_i", [128, 128], I32)
        gcols = cst("gcols", [128, 16], F32)
        bcols_bf = cst("bcols_bf", [128, 16], BF16)
        fb2 = cst("fb2", [128, 16], F32)
        babcol = cst("babcol", [128, 1], F32)
        cwc = cst("cwc", [128, 48], F32)
        d_col = cst("d_col", [128, 16], F32)
        rg_col = cst("rg_col", [128, 16], F32)
        thr2 = cst("thr2", [128, 2], F32)
        ABSC = cst("ABSC", [128, 17], F32)
        ABS16 = cst("ABS16", [128, 16], F32)
        ABS64 = cst("ABS64", [128, 64], F32)
        assert CO <= 191488, CO
        statrow = nc.alloc_sbuf_tensor_at("statrow", [1, 4096], F32,
                                          offset=SB_BASE + 191552)
        zeros128 = nc.alloc_sbuf_tensor_at("zeros128", [128, 128], F32,
                                           offset=SB_BASE + 211008)
        ROW_SUM, ROW_SQ, ROW_T1, ROW_T2 = 0, 1024, 2048, 3072
        SB_SN, SB_UP = 0, 1024

        X = M.at("X", 0, [128, NK * TOK], F32)
        _tailref["X"] = X
        XB = M.at("XB", 65536, [128, NK * TOK], BF16)
        WGAM = M.at("WGAM", 98304, [128, NK * D], BF16)
        XN1 = nc.alloc_sbuf_tensor_at("XN1", [128, NK * TOK], BF16,
                                      offset=SB_BASE + 98304)
        WFA = M.at("WFA", 163840, [128, D], F32)
        WFB = nc.alloc_sbuf_tensor_at("WFB", [128, D], F32,
                                      offset=SB_BASE + 163840 + 8192)
        WFR = [WFA, WFB]
        QSC = M.at("QSC", 180224, [128, D], BF16)
        X2R = nc.alloc_sbuf_tensor_at("X2R", [128, 2 * TOK], BF16,
                                      offset=SB_BASE + 180224)
        SBC = nc.alloc_sbuf_tensor_at("SBC", [128, TOK], F32,
                                      offset=SB_BASE + 163840 + 8192)
        UBC = nc.alloc_sbuf_tensor_at("UBC", [128, TOK], F32,
                                      offset=SB_BASE + 163840 + 12288)

        try:
            P.run("gpsimd", lambda g: g.iota(iota_i[:], [[-1, 128]], base=0,
                                             channel_multiplier=1))
            vG = P.sig("gpsimd")

            def c_masks(v):
                v.tensor_single_scalar(ident[:], iota_i[:], 0, ALU.is_equal)
                v.tensor_single_scalar(strictL[:], iota_i[:], 0, ALU.is_gt)
                v.tensor_single_scalar(inclU[:], iota_i[:], 0, ALU.is_le)
                v.memset(ones_r_bf[:], 1.0)
                v.memset(ones_r_f[:], 1.0)
                v.memset(ones_c_bf[:], 1.0)
                v.memset(ones_c_f[:], 1.0)
                v.memset(zeros128[:], 0.0)
            P.wait("vector", "G", vG)
            P.run("vector", c_masks)
            vconsts = P.sig("vector")

            P.dma("sync", lambda e: e.dma_start(
                out=X[:].rearrange("p (k t) -> p k t", k=NK),
                in_=xT.rearrange("(k p) t -> p k t", p=128)))
            dX = P.cnt["DS"]
            P.dma("sync", lambda e: e.dma_start(out=gcols[:], in_=gcols_d))
            P.dma("sync", lambda e: e.dma_start(out=fb2[:], in_=fbcols_d))
            P.dma("sync", lambda e: e.dma_start(out=babcol[:], in_=babcol_d))
            P.dma("sync", lambda e: e.dma_start(out=cwc[:], in_=cw48))
            coreoff_sb = nc.alloc_sbuf_tensor_at("coreoff_sb", [1, 1], I32,
                                                 offset=SB_BASE + 212032)
            P.dma("sync", lambda e: e.dma_start(out=coreoff_sb[:], in_=coreoff_d))
            dSmall = P.cnt["DS"]
            P.dma("gpsimd", lambda e: e.dma_start(
                out=XB[:].rearrange("p (k t) -> p k t", k=NK),
                in_=xT.rearrange("(k p) t -> p k t", p=128)))
            dXB = P.cnt["DG"]
            P.dma("gpsimd", lambda e: e.dma_start(out=bcols_bf[:], in_=bcols_d))
            dBC = P.cnt["DG"]

            _ck(10)
            vred = {}
            dWF = {}
            for k in range(16):
                if k >= 1:
                    P.wait("sync", "V", vred[k - 1])

                def d_wf(e, k=k):
                    return e.dma_start(out=WFR[k % 2][:],
                                       in_=wfT[128 * k:128 * k + 128, :])
                dWF[k] = P.dma("sync", d_wf)
                P.wait("vector", "DS", dWF[k])

                def redk(v, k=k):
                    for j in range(4):
                        v.tensor_reduce(ABS64[:, 4 * k + j:4 * k + j + 1],
                                        WFR[k % 2][:, 512 * j:512 * j + 512],
                                        AX.X, ALU.add, apply_absolute_value=True)
                P.run("vector", redk)
                vred[k] = P.sig("vector")

            P.run("vector", lambda v: v.tensor_reduce(
                ABSC[:, 16:17], ABS64[:], AX.X, ALU.add))
            vabs = P.sig("vector")

            P.wait("sync", "V", vabs)
            dGat = P.dma("sync", lambda e: e.dma_start(
                out=statrow[0:1, 128:256], in_=ABSC[:, 16:17]))
            P.wait("vector", "DS", dGat)
            P.run("vector", lambda v: v.tensor_reduce(
                statrow[0:1, 256:257], statrow[0:1, 128:256], AX.X, ALU.add))
            vTot = P.sig("vector")

            def s_thr(s):
                s.activation(statrow[0:1, 0:1], statrow[0:1, 256:257],
                             AF.Identity, scale=0.5 / (D * D))
            P.wait("scalar", "V", vTot)
            P.run("scalar", s_thr)
            sThr = P.sig("scalar")

            def v_nthr(v):
                v.tensor_single_scalar(statrow[0:1, 0:1], statrow[0:1, 0:1],
                                       0.5e-5, ALU.add)
                v.drain()
                v.tensor_single_scalar(statrow[0:1, 1:2], statrow[0:1, 0:1],
                                       -1.0, ALU.mult)
            P.wait("vector", "S", sThr)
            P.run("vector", v_nthr)
            vNthr = P.sig("vector")

            P.wait("sync", "V", vNthr)
            dtw = P.dma("sync", lambda e: e.dma_start(
                out=bc_scr[0:1, 2 * TOK:2 * TOK + 2], in_=statrow[0:1, 0:2]))
            P.wait("sync", "DS", dtw)
            dThr = P.dma("sync", lambda e: e.dma_start(
                out=thr2[:], in_=bass.AP(bc_scr.tensor, 2 * TOK, [[0, 128], [1, 2]])))

            _ck(20)
            with nc.psum_tensor("ps_wb", [1, D], F32) as ps_wb, \
                 nc.psum_tensor("ps_rg", [1, D], F32) as ps_rg:
                gq = {}
                tQS = 0
                for k in range(16):
                    if k >= 1:
                        P.wait("sync", "V", gq[k - 1])

                    def d_wf2(e, k=k):
                        return e.dma_start(out=WFR[k % 2][:],
                                           in_=wfT[128 * k:128 * k + 128, :])
                    dQk = P.dma("sync", d_wf2)
                    src = WFR[k % 2][:]
                    P.wait("vector", "DS", dQk)
                    if k == 0:
                        P.wait("vector", "DS", dSmall)
                        P.wait("vector", "DS", dThr)
                        P.wait("vector", "DG", dBC)
                    if tQS:
                        P.wait("vector", "T", tQS)

                    def v_isgt(v, src=src):
                        v.tensor_single_scalar(QSC[:], src, thr2[:, 0:1], ALU.is_gt)
                    P.run("vector", v_isgt)
                    vq = P.sig("vector")

                    def g_stt(v, src=src):
                        v.scalar_tensor_tensor(QSC[:], src, thr2[:, 1:2], QSC[:],
                                               ALU.is_lt, ALU.subtract)
                    P.run("vector", g_stt)
                    gq[k] = P.sig("vector")

                    P.wait("tensor", "V", gq[k])

                    def mm_wb(t, k=k):
                        for j in range(4):
                            t.matmul(ps_wb[:, 512 * j:512 * j + 512],
                                     bcols_bf[:, k:k + 1],
                                     QSC[:, 512 * j:512 * j + 512],
                                     start=(k == 0), stop=(k == 15))
                        t.nop()
                    P.run("tensor", mm_wb)
                    tQS = P.sig("tensor")

                    def v_fold(v, k=k):
                        v.tensor_scalar_mul(WGAM[:, D * k:D * k + D], QSC[:],
                                            gcols[:, k:k + 1])
                    P.run("vector", v_fold)
                    vwg = P.sig("vector")

                    P.wait("tensor", "V", vwg)

                    def mm_rg(t, k=k):
                        for j in range(4):
                            t.matmul(ps_rg[:, 512 * j:512 * j + 512], ones_c_bf[:],
                                     WGAM[:, D * k + 512 * j:D * k + 512 * j + 512],
                                     start=(k == 0), stop=(k == 15))
                        t.nop()
                    P.run("tensor", mm_rg)
                tWB = P.sig("tensor")

                P.wait("sync", "T", tWB)
                dWG = P.dma("sync", lambda e: e.dma_start(
                    out=wgam_scr.rearrange("(k p) o -> p k o", p=128),
                    in_=WGAM[:].rearrange("p (k o) -> p k o", k=NK)))

                P.wait("vector", "T", tWB)
                P.run("vector", lambda v: v.tensor_copy(
                    statrow[0:1, ROW_SUM:ROW_SUM + 2048], ps_wb[0:1, :]))
                vWBr = P.sig("vector")
                P.run("vector", lambda v: v.tensor_copy(
                    statrow[0:1, ROW_T1:ROW_T1 + 2048], ps_rg[0:1, :]))
                vRGr = P.sig("vector")

            with nc.psum_tensor("ps_c16", [128, 32], F32) as ps_c16:
                def mm_cols(t):
                    for k in range(16):
                        t.matmul(ps_c16[:, k:k + 1],
                                 statrow[0:1, ROW_SUM + 128 * k:ROW_SUM + 128 * k + 128],
                                 ones_r_f[0:1, 0:1], start=True, stop=True)
                        t.matmul(ps_c16[:, 16 + k:16 + k + 1],
                                 statrow[0:1, ROW_T1 + 128 * k:ROW_T1 + 128 * k + 128],
                                 ones_r_f[0:1, 0:1], start=True, stop=True)
                    t.nop()
                P.wait("tensor", "V", vRGr)
                P.run("tensor", mm_cols)
                tCols = P.sig("tensor")

                def v_cols(v):
                    v.tensor_scalar(fb2[:], fb2[:], 0.5, None, ALU.mult)
                    v.drain()
                    v.scalar_tensor_tensor(d_col[:], ps_c16[:, 0:16], -0.5, fb2[:],
                                           ALU.mult, ALU.add)
                    v.tensor_copy(rg_col[:], ps_c16[:, 16:32])
                P.wait("vector", "T", tCols)
                P.run("vector", v_cols)
                vDC = P.sig("vector")

            _ck(30)
            def ln_stats(tag, sq_wait_sem, sq_wait_vals, sum_rhs, inner_ln,
                         sum_wait=None, first_sq_waits=(), guard_v=0, xsrc=None):
                with nc.psum_tensor("ps_s" + tag, [1, TOK], F32) as ps_s, \
                     nc.psum_tensor("ps_q" + tag, [1, TOK], F32) as ps_q:
                    tprev = 0
                    for k in range(16):
                        P.wait("scalar", sq_wait_sem, sq_wait_vals[k])
                        if k == 0:
                            for (snm, sval) in first_sq_waits:
                                P.wait("scalar", snm, sval)
                        if tprev:
                            P.wait("scalar", "T", tprev)

                        def s_sq(s, k=k, xs=xsrc):
                            xs = X if xs is None else xs
                            s.activation(X2R[:, (k % 2) * TOK:(k % 2) * TOK + TOK],
                                         xs[:, TOK * k:TOK * k + TOK], AF.Square)
                        P.run("scalar", s_sq)
                        sq_k = P.sig("scalar")

                        P.wait("tensor", "S", sq_k)
                        if sum_wait is not None:
                            P.wait("tensor", sum_wait[0], sum_wait[1][k])

                        def mm_st(t, k=k):
                            for j in range(2):
                                t.matmul(ps_s[:, 512 * j:512 * j + 512], ones_c_bf[:],
                                         sum_rhs(k, j), start=(k == 0), stop=(k == 15))
                                t.matmul(ps_q[:, 512 * j:512 * j + 512], ones_c_bf[:],
                                         X2R[:, (k % 2) * TOK + 512 * j:
                                              (k % 2) * TOK + 512 * j + 512],
                                         start=(k == 0), stop=(k == 15))
                            t.nop()
                        P.run("tensor", mm_st)
                        tprev = P.sig("tensor")

                    def v_stat(v):
                        v.tensor_copy(statrow[0:1, ROW_SUM:ROW_SUM + TOK], ps_s[0:1, :])
                        v.tensor_copy(statrow[0:1, ROW_SQ:ROW_SQ + TOK], ps_q[0:1, :])
                        v.tensor_mul(statrow[0:1, ROW_T1:ROW_T1 + TOK],
                                     statrow[0:1, ROW_SUM:ROW_SUM + TOK],
                                     statrow[0:1, ROW_SUM:ROW_SUM + TOK])
                        if inner_ln:
                            k1 = (1.0 + LN_EPS) / D
                            eps2 = LN_EPS + LN_EPS * LN_EPS
                        else:
                            k1 = 1.0 / D
                            eps2 = LN_EPS
                        v.tensor_scalar(statrow[0:1, ROW_T2:ROW_T2 + TOK],
                                        statrow[0:1, ROW_SQ:ROW_SQ + TOK], k1, eps2,
                                        ALU.mult, ALU.add)
                        v.scalar_tensor_tensor(statrow[0:1, ROW_T2:ROW_T2 + TOK],
                                               statrow[0:1, ROW_T1:ROW_T1 + TOK],
                                               -k1 / D,
                                               statrow[0:1, ROW_T2:ROW_T2 + TOK],
                                               ALU.mult, ALU.add)
                        v.reciprocal(statrow[0:1, ROW_T1:ROW_T1 + TOK],
                                     statrow[0:1, ROW_T2:ROW_T2 + TOK])
                    P.wait("vector", "T", tprev)
                    P.run("vector", v_stat)
                    vst = P.sig("vector")

                def s_stat(s):
                    s.activation(statrow[0:1, ROW_T2:ROW_T2 + TOK],
                                 statrow[0:1, ROW_T1:ROW_T1 + TOK], AF.Sqrt)
                    s.activation(statrow[0:1, ROW_SQ:ROW_SQ + TOK],
                                 statrow[0:1, ROW_T2:ROW_T2 + TOK], AF.Identity,
                                 scale=(-0.5 if inner_ln else 1.0))
                P.wait("scalar", "V", vst)
                P.run("scalar", s_stat)
                sst = P.sig("scalar")

                def v_statb(v):
                    v.tensor_mul(statrow[0:1, ROW_T1:ROW_T1 + TOK],
                                 statrow[0:1, ROW_SUM:ROW_SUM + TOK],
                                 statrow[0:1, ROW_T2:ROW_T2 + TOK])
                    v.tensor_scalar(statrow[0:1, ROW_T1:ROW_T1 + TOK],
                                    statrow[0:1, ROW_T1:ROW_T1 + TOK],
                                    (0.5 / D if inner_ln else -1.0 / D), None,
                                    ALU.mult)
                P.wait("vector", "S", sst)
                P.run("vector", v_statb)
                vstb = P.sig("vector")

                P.wait("sync", "V", vstb)
                P.dma("sync", lambda e: e.dma_start(
                    out=bc_scr[0:1, 0:TOK], in_=statrow[0:1, ROW_SQ:ROW_SQ + TOK]))
                dw2 = P.dma("sync", lambda e: e.dma_start(
                    out=bc_scr[0:1, TOK:2 * TOK],
                    in_=statrow[0:1, ROW_T1:ROW_T1 + TOK]))
                P.wait("sync", "DS", dw2)
                if guard_v:
                    P.wait("sync", "V", guard_v)
                P.dma("sync", lambda e: e.dma_start(
                    out=SBC[:], in_=bass.AP(bc_scr.tensor, 0, [[0, 128], [1, TOK]])))
                dbc = P.dma("sync", lambda e: e.dma_start(
                    out=UBC[:], in_=bass.AP(bc_scr.tensor, TOK, [[0, 128], [1, TOK]])))
                return dbc

            def ffn(tag, wait_T_extra=0, wgam=None, xb=None, xdst=None):
                wgam = WGAM if wgam is None else wgam
                xb = XB if xb is None else xb
                xdst = X if xdst is None else xdst
                with nc.psum_tensor("ps_f" + tag, [128, 8 * 512], F32) as ps_f:
                    vEv = 0
                    for t2 in range(2):
                        for og in range(2):
                            if vEv:
                                P.wait("tensor", "V", vEv)
                            if wait_T_extra and t2 == 0 and og == 0:
                                pass
                            for o in range(8):
                                oc = og * 8 + o

                                def mm_f(t, t2=t2, oc=oc, o=o, wgam=wgam, xb=xb):
                                    for k in range(16):
                                        t.matmul(
                                            ps_f[:, 512 * o:512 * o + 512],
                                            wgam[:, D * k + 128 * oc:D * k + 128 * oc + 128],
                                            xb[:, TOK * k + 512 * t2:TOK * k + 512 * t2 + 512],
                                            start=(k == 0), stop=(k == 15))
                                    t.nop()
                                P.run("tensor", mm_f)
                                tF = P.sig("tensor")

                                P.wait("vector", "T", tF)

                                def v_ep(v, t2=t2, oc=oc, o=o, xdst=xdst):
                                    ts = 512 * t2
                                    pslc = ps_f[:, 512 * o:512 * o + 512]
                                    xs = xdst[:, TOK * oc + ts:TOK * oc + ts + 512]
                                    t1 = WFA[:, 0:512]
                                    v.tensor_mul(t1, pslc, SBC[:, ts:ts + 512])
                                    v.scalar_tensor_tensor(
                                        t1, UBC[:, ts:ts + 512], rg_col[:, oc:oc + 1],
                                        t1, ALU.mult, ALU.add)
                                    v.scalar_tensor_tensor(
                                        xs, t1, d_col[:, oc:oc + 1], xs, ALU.add, ALU.add)
                                P.run("vector", v_ep)
                                vEv = P.sig("vector")
                    return vEv

            vBC1 = ln_stats("1", "DS", {k: dX for k in range(16)},
                            lambda k, j: XB[:, TOK * k + 512 * j:TOK * k + 512 * j + 512],
                            inner_ln=True,
                            sum_wait=("DG", {k: dXB for k in range(16)}),
                            first_sq_waits=(("T", tCols), ("V", vDC)))
            _ck(40)
            if core_dbg:
                P.wait("sync", "V", vDC)
                P.dma("sync", lambda e: e.dma_start(
                    out=dbg_sb[0:1, 0:2], in_=thr2[0:1, 0:2]))
                P.dma("sync", lambda e: e.dma_start(
                    out=dbg_sb[1:3, 0:2], in_=thr2[100:102, 0:2]))
                P.wait("gpsimd", "DS", dWG)
                P.dma("gpsimd", lambda e: e.dma_start(out=dbg_wg, in_=wgam_scr))
                P.wait("gpsimd", "DS", vBC1)
                dDump = P.dma("gpsimd", lambda e: e.dma_start(
                    out=dbg_xb.rearrange("(k p) t -> p k t", p=128),
                    in_=XB[:].rearrange("p (k t) -> p k t", k=NK)))
                P.wait("vector", "DG", dDump)
            P.wait("tensor", "DG", dXB)
            P.wait("tensor", "V", vDC)
            P.wait("tensor", "DS", vBC1)
            vFFN1 = ffn("1")

            _ck(50)
            P.wait("sync", "V", vFFN1)
            P.dma("sync", lambda e: e.dma_start(
                out=xmid_scr.rearrange("(k p) t -> p k t", p=128),
                in_=X[:].rearrange("p (k t) -> p k t", k=NK)))
            if core_dbg:
                P.dma("sync", lambda e: e.dma_start(
                    out=dbg.rearrange("(k p) t -> p k t", p=128),
                    in_=X[:].rearrange("p (k t) -> p k t", k=NK)))

            vxb2 = {}
            for k in range(16):
                if k == 0:
                    P.wait("vector", "V", vFFN1)

                def v_xb2(v, k=k):
                    v.tensor_copy(XB[:, TOK * k:TOK * k + TOK], X[:, TOK * k:TOK * k + TOK])
                P.run("vector", v_xb2)
                vxb2[k] = P.sig("vector")

            sq_waits2 = {k: vFFN1 for k in range(16)}
            vBC2 = ln_stats("2", "V", sq_waits2,
                            lambda k, j: XB[:, TOK * k + 512 * j:TOK * k + 512 * j + 512],
                            inner_ln=False,
                            sum_wait=("V", vxb2), guard_v=vFFN1)
            vXN = 0
            for k in range(16):
                if k == 0:
                    P.wait("vector", "DS", vBC2)
                    P.wait("vector", "DS", dWG)

                def v_xn(v, k=k):
                    t1 = WFA[:, 0:TOK]
                    v.tensor_mul(t1, X[:, TOK * k:TOK * k + TOK], SBC[:])
                    v.tensor_add(XN1[:, TOK * k:TOK * k + TOK], t1, UBC[:])
                P.run("vector", v_xn)
                vXN = P.sig("vector")

            _ck(60)
            P.wait("sync", "V", vXN)
            P.dma("sync", lambda e: e.dma_start(
                out=ag1_in.rearrange("(k p) t -> p k t", p=128),
                in_=XN1[:].rearrange("p (k t) -> p k t", k=NK)))
            dAG1 = P.cnt["DS"]

            P.wait("gpsimd", "DS", dAG1)
            ccAG1 = P.cc(lambda g: g.collective_compute(
                "AllGather", ALU.bypass,
                replica_groups=[[0, 1, 2, 3], [4, 5, 6, 7]],
                ins=[ag1_in], outs=[ag1_out]))

            if core_dbg:
                P.wait("gpsimd", "V", vXN)
                P.dma("gpsimd", lambda e: e.dma_start(
                    out=dbg2.rearrange("(k p) t -> p k t", p=128),
                    in_=XN1[:].rearrange("p (k t) -> p k t", k=NK)))


            WQKV = nc.alloc_sbuf_tensor_at("WQKV", [128, 16 * 1536], BF16, offset=SB_BASE + 0)
            WGT = nc.alloc_sbuf_tensor_at("WGT", [128, 16 * 512], BF16, offset=SB_BASE + 49152)
            WAB = nc.alloc_sbuf_tensor_at("WAB", [128, 16 * 128], BF16, offset=SB_BASE + 65536)
            XNT = [nc.alloc_sbuf_tensor_at("XNT0", [128, 16 * 512], BF16, offset=SB_BASE + 69632),
                   nc.alloc_sbuf_tensor_at("XNT1", [128, 16 * 512], BF16, offset=SB_BASE + 86016)]
            ZR = nc.alloc_sbuf_tensor_at("ZR", [128, 12 * 515], BF16, offset=SB_BASE + 102400)
            SILR = [nc.alloc_sbuf_tensor_at("SIL0", [128, 12 * 512], BF16, offset=SB_BASE + 114784),
                    nc.alloc_sbuf_tensor_at("SIL1", [128, 12 * 512], BF16, offset=SB_BASE + 127072)]
            GSIL = nc.alloc_sbuf_tensor_at("GSIL", [128, 4 * 512], BF16, offset=SB_BASE + 139360)
            VECF = nc.alloc_sbuf_tensor_at("VECF", [128, 2048], F32, offset=SB_BASE + 143456)
            VECB = nc.alloc_sbuf_tensor_at("VECB", [128, 3072], BF16, offset=SB_BASE + 151648)
            CONVA = nc.alloc_sbuf_tensor_at("CONVA", [128, 512], F32, offset=SB_BASE + 157792)
            SCR = nc.alloc_sbuf_tensor_at("SCR", [128, 20 * 512], BF16, offset=SB_BASE + 159840)
            UBCT = nc.alloc_sbuf_tensor_at("UBCT", [128, 512], BF16, offset=SB_BASE + 180320)
            ACB = nc.alloc_sbuf_tensor_at("ACB", [128, 512], BF16, offset=SB_BASE + 181344)
            StF = nc.alloc_sbuf_tensor_at("StF", [128, 512], F32, offset=SB_BASE + 182368)
            StB = nc.alloc_sbuf_tensor_at("StB", [128, 512], BF16, offset=SB_BASE + 184416)
            GATED = nc.alloc_sbuf_tensor_at("GATED", [128, 512], BF16, offset=SB_BASE + 185440)
            ACROW = nc.alloc_sbuf_tensor_at("ACROW", [1, 512], BF16, offset=SB_BASE + 186464)
            RP1 = nc.alloc_sbuf_tensor_at("RP1", [1, 6 * 4 * 256], BF16,
                                          offset=SB_BASE + 191552)
            RP2 = nc.alloc_sbuf_tensor_at("RP2", [1, 2 * 4 * 256], BF16,
                                          offset=SB_BASE + 203840)
            AC_OFF, RINS_OFF, BB_OFF = 512, 1024, 1536
            AINV_OFF, RK_OFF, RQ_OFF, C1_OFF, C2_OFF, C3_OFF = 0, 512, 1024, 1536, 2048, 2560
            iM, iMT = 0, 1
            iP0, iP1, iP2x, iP3 = 2, 3, 4, 5
            iF, iZ0, iZ1, iTINV = 6, 7, 8, 9
            iSCRT, iQAT, iRV, iRBK, iKT2 = 10, 11, 12, 13, 14
            iPP, iGT, iUU, iUT, iFMT = 15, 16, 17, 18, 19

            def scr(i):
                return SCR[:, 512 * i:512 * i + 512]

            STRICT4 = nc.alloc_sbuf_tensor_at("STRICT4", [128, 512], BF16,
                                              offset=SB_BASE + 159840 + 1024 * 4)
            INCL4 = nc.alloc_sbuf_tensor_at("INCL4", [128, 512], BF16,
                                            offset=SB_BASE + 159840 + 1024 * 5)
            IDENT4 = nc.alloc_sbuf_tensor_at("IDENT4", [128, 512], BF16,
                                             offset=SB_BASE + 159840 + 1024 * 9)
            STRICTU4 = nc.alloc_sbuf_tensor_at("STRICTU4", [128, 512], BF16,
                                               offset=SB_BASE + 207936)
            TINYROW = nc.alloc_sbuf_tensor_at("TINYROW", [1, 512], BF16,
                                              offset=SB_BASE + 208960)
            SQB = nc.alloc_sbuf_tensor_at("SQB", [128, 512], BF16,
                                          offset=SB_BASE + 209984)

            def c_p1consts(v):
                v.tensor_single_scalar(WFA[:, 640:768], iota_i[:], 0, ALU.is_lt)
                v.drain()
                for h in range(4):
                    o = 128 * h
                    v.tensor_copy(STRICT4[:, o:o + 128], strictL[:])
                    v.tensor_copy(INCL4[:, o:o + 128], inclU[:])
                    v.tensor_copy(IDENT4[:, o:o + 128], ident[:])
                    v.tensor_copy(STRICTU4[:, o:o + 128], WFA[:, 640:768])
                v.memset(TINYROW[:], 1e-30)
                v.memset(StF[:], 0.0)
                v.memset(StB[:], 0.0)
            P.run("vector", c_p1consts)

            vP1C = P.sig("vector")

            P.wait("gpsimd", "V", vXN)
            P.wait("gpsimd", "DS", dAG1)
            P.dma("gpsimd", lambda e: e.dma_start(
                out=WQKV[:].rearrange("p (k o) -> p k o", k=NK),
                in_=wqkvT.rearrange("(k p) o -> p k o", p=128)))
            P.dma("gpsimd", lambda e: e.dma_start(
                out=WGT[:].rearrange("p (k o) -> p k o", k=NK),
                in_=wgT.rearrange("(k p) o -> p k o", p=128)))
            dW1 = P.dma("gpsimd", lambda e: e.dma_start(
                out=WAB[:].rearrange("p (k o) -> p k o", k=NK),
                in_=wabsp.rearrange("(k p) o -> p k o", p=128)))

            P.wait("vector", "DS", dAG1)

            def v_zinit(v):
                v.memset(ZR[:].rearrange("p (c w) -> p c w", c=12)[:, :, 0:3], 0.0)
                v.memset(VECF[:], 0.0)
                v.memset(VECB[:], 0.0)
            P.run("vector", v_zinit)
            vZI = P.sig("vector")

            with nc.psum_tensor("psQ", [128, 3 * 512], F32) as psQ, \
                 nc.psum_tensor("psAB", [128, 512], F32) as psAB, \
                 nc.psum_tensor("psG", [128, 512], F32) as psG, \
                 nc.psum_tensor("psO1", [128, 512], F32) as psO1, \
                 nc.psum_tensor("psTR", [128, 1536], BF16) as psTR:
                tQ = {}
                vEvQ = {}
                vConv = {}
                sSil = {}
                sGsil = {}
                last_t = 0
                last_v = 0

                def TG(fn):
                    nonlocal last_t
                    P.wait("tensor", "V", last_v)
                    P.run("tensor", fn)
                    last_t = P.sig("tensor")
                    return last_t

                def VG(fn):
                    nonlocal last_v
                    P.wait("vector", "T", last_t)
                    P.run("vector", fn)
                    last_v = P.sig("vector")
                    return last_v

                def rp1s(sec, h, c, n=128):
                    base = h * (6 * 256) + sec * 256 + (c % 2) * 128
                    return RP1[0:1, base:base + n]

                def rp2s(sec, h, c, n=128):
                    base = h * (2 * 256) + sec * 256 + (c % 2) * 128
                    return RP2[0:1, base:base + n]

                def silg(buf, grp, c):
                    cc0 = {"q": 0, "k": 4, "v": 8}[grp]
                    r = SILR[buf][:].rearrange("p (cc w) -> p cc w", cc=12)
                    return r[:, cc0:cc0 + 4, 128 * c:128 * c + 128]

                def sils(buf, grp, h, c):
                    cc0 = {"q": 0, "k": 4, "v": 8}[grp]
                    off = 512 * (cc0 + h) + 128 * c
                    return SILR[buf][:, off:off + 128]

                ps4 = lambda ps: ps[:].rearrange("p (h w) -> p h w", h=4)

                for T in range(NT):
                    buf = T % 2
                    bblk = T // 2
                    toff = 512 * (T % 2)
                    if T == 0:
                        P.wait("sync", "CC", ccAG1)
                    if T >= 2:
                        P.wait("sync", "T", tQ[(T - 2, 12)])
                    dXN = P.dma("sync", lambda e, buf=buf, bblk=bblk, toff=toff: e.dma_start(
                        out=XNT[buf][:].rearrange("p (k t) -> p k t", k=NK),
                        in_=ag1_out[bblk, :, toff:toff + 512].rearrange(
                            "(k p) t -> p k t", p=128)))

                    P.wait("tensor", "DS", dXN)
                    if T == 0:
                        P.wait("tensor", "DG", dW1)
                        P.wait("tensor", "V", vP1C)
                    for oc in range(12):
                        bank = oc % 3
                        if (T, oc - 3) in vEvQ:
                            P.wait("tensor", "V", vEvQ[(T, oc - 3)])
                        elif T > 0 and oc < 3:
                            P.wait("tensor", "V", vEvQ[(T - 1, 9 + oc)])

                        def mm_q(t, buf=buf, oc=oc, bank=bank):
                            for k in range(16):
                                t.matmul(psQ[:, 512 * bank:512 * bank + 512],
                                         WQKV[:, 1536 * k + 128 * oc:1536 * k + 128 * oc + 128],
                                         XNT[buf][:, 512 * k:512 * k + 512],
                                         start=(k == 0), stop=(k == 15))
                        P.run("tensor", mm_q)
                        tQ[(T, oc)] = P.sig("tensor")
                        zslc = ZR[:, 515 * oc + 3:515 * oc + 515]
                        if oc % 2 == 0:
                            P.wait("vector", "T", tQ[(T, oc)])
                            if T > 0:
                                P.wait("vector", "V", vConv[(T - 1, oc)])

                            def v_ev(v, zslc=zslc, bank=bank):
                                v.tensor_copy(zslc, psQ[:, 512 * bank:512 * bank + 512])
                            P.run("vector", v_ev)
                            vEvQ[(T, oc)] = P.sig("vector")
                        else:
                            P.wait("scalar", "T", tQ[(T, oc)])
                            if T == 0:
                                P.wait("scalar", "V", vZI)
                            if T > 0:
                                P.wait("scalar", "V", vConv[(T - 1, oc)])

                            def s_ev(s, zslc=zslc, bank=bank):
                                s.copy(zslc, psQ[:, 512 * bank:512 * bank + 512])
                            P.run("scalar", s_ev)
                            sev = P.sig("scalar")
                            P.wait("vector", "S", sev)
                            vEvQ[(T, oc)] = P.sig("vector")

                    def mm_ab(t, buf=buf):
                        for k in range(16):
                            t.matmul(psAB[:], WAB[:, 128 * k:128 * k + 128],
                                     XNT[buf][:, 512 * k:512 * k + 512],
                                     start=(k == 0), stop=(k == 15))
                    if T > 0:
                        P.wait("tensor", "V", last_v)
                    P.run("tensor", mm_ab)
                    tQ[(T, 12)] = P.sig("tensor")

                    P.wait("scalar", "T", tQ[(T, 12)])
                    if T == 0:
                        P.wait("scalar", "V", vZI)

                    def s_sig(s):
                        s.activation(VECF[:, 0:512], psAB[:], AF.Sigmoid,
                                     bias=babcol[:])
                    P.run("scalar", s_sig)
                    sAB = P.sig("scalar")

                    P.wait("vector", "S", sAB)

                    def v_acum(v):
                        for j in range(4):
                            v.tensor_tensor_scan(
                                VECF[:, AC_OFF + 128 * j:AC_OFF + 128 * j + 128],
                                VECF[:, 128 * j:128 * j + 128],
                                zeros128[:], 1.0, ALU.mult, ALU.add)
                        v.drain()
                        v.tensor_scalar(VECF[:, AC_OFF:AC_OFF + 512],
                                        VECF[:, AC_OFF:AC_OFF + 512], 1e-30, None,
                                        ALU.max)
                        v.drain()
                        v.reciprocal(VECB[:, AINV_OFF:AINV_OFF + 512],
                                     VECF[:, AC_OFF:AC_OFF + 512])
                    P.run("vector", v_acum)
                    vACm = P.sig("vector")

                    P.wait("sync", "V", vACm)
                    for h_ in range(4):
                        dBB = P.dma("sync", lambda e, h_=h_: e.dma_start(
                            out=VECF[32 * h_:32 * h_ + 1, BB_OFF:BB_OFF + 512],
                            in_=VECF[32 * h_ + 16:32 * h_ + 17, 0:512]))

                    for cc in range(12):
                        P.wait("vector", "V", vEvQ[(T, cc)])
                        if T == 0 and cc == 0:
                            P.wait("vector", "DS", dSmall)

                        def v_conv(v, cc=cc):
                            zb = 515 * cc
                            v.tensor_scalar(CONVA[:], ZR[:, zb:zb + 512],
                                            cwc[:, 4 * cc:4 * cc + 1], None, ALU.mult)
                            v.drain()
                            for tau in range(1, 4):
                                v.scalar_tensor_tensor(
                                    CONVA[:], ZR[:, zb + tau:zb + tau + 512],
                                    cwc[:, 4 * cc + tau:4 * cc + tau + 1],
                                    CONVA[:], ALU.mult, ALU.add)
                                v.drain()
                            v.tensor_copy(ZR[:, zb:zb + 3], ZR[:, zb + 512:zb + 515])
                        P.run("vector", v_conv)
                        vConv[(T, cc)] = P.sig("vector")

                        P.wait("scalar", "V", vConv[(T, cc)])

                        def s_sil(s, cc=cc, buf=buf):
                            s.activation(SILR[buf][:, 512 * cc:512 * cc + 512],
                                         CONVA[:], AF.Silu)
                        P.run("scalar", s_sil)
                        sSil[(T, cc)] = P.sig("scalar")

                    for j in range(4):
                        if j == 0:
                            P.wait("tensor", "S", sSil[(T, 11)])
                            if T > 0:
                                P.wait("tensor", "V", last_v)
                        else:
                            P.wait("tensor", "S", sGsil[(T, j - 1)])

                        def mm_g(t, buf=buf, j=j):
                            for k in range(16):
                                t.matmul(psG[:],
                                         XNT[buf][:, 512 * k + 128 * j:512 * k + 128 * j + 128],
                                         WGT[:, 512 * k:512 * k + 512],
                                         start=(k == 0), stop=(k == 15))
                        P.run("tensor", mm_g)
                        tg = P.sig("tensor")
                        P.wait("scalar", "T", tg)

                        def s_gsil(s, j=j):
                            s.activation(GSIL[:, 512 * j:512 * j + 512], psG[:], AF.Silu)
                        P.run("scalar", s_gsil)
                        sGsil[(T, j)] = P.sig("scalar")
                    last_t = tg

                    for grp, off in (("k", RK_OFF), ("q", RQ_OFF)):
                        P.wait("tensor", "S", sGsil[(T, 3)])
                        if grp == "q":
                            P.wait("tensor", "V", last_v)

                        def mm_zf(t):
                            t.matmul(psAB[:], ones_r_bf[:], TINYROW[:],
                                     start=True, stop=False, skip_group_check=True)
                        P.run("tensor", mm_zf)
                        tzf = P.sig("tensor")
                        for h in range(4):
                            if h == 0:
                                P.wait("scalar", "V", vConv[(T, 11)])
                                P.wait("scalar", "T", tzf)

                            def s_sq(s, h=h, grp=grp, buf=buf):
                                cc = (4 + h) if grp == "k" else h
                                s.activation(SQB[:], SILR[buf][:, 512 * cc:512 * cc + 512],
                                             AF.Square)
                            P.run("scalar", s_sq)
                            ssq_h = P.sig("scalar")
                            P.wait("tensor", "S", ssq_h)

                            def mm_ssq(t, h=h):
                                t.matmul(psAB[32 * h:32 * h + 1, :], ones_c_bf[:],
                                         SQB[:], start=False, stop=(h == 3),
                                         tile_position=(0, 32 * h),
                                         skip_group_check=True)
                            P.run("tensor", mm_ssq)
                        last_t = P.sig("tensor")

                        P.wait("vector", "T", last_t)

                        def v_rins(v):
                            v.reciprocal(VECF[:, RINS_OFF:RINS_OFF + 512], psAB[:])
                        P.run("vector", v_rins)
                        vri = P.sig("vector")
                        P.wait("scalar", "V", vri)

                        def s_rsq(s):
                            s.activation(VECF[:, RINS_OFF:RINS_OFF + 512],
                                         VECF[:, RINS_OFF:RINS_OFF + 512], AF.Sqrt)
                        P.run("scalar", s_rsq)
                        srs = P.sig("scalar")
                        P.wait("vector", "S", srs)

                        def v_rmin(v, off=off):
                            v.tensor_scalar(VECB[:, off:off + 512],
                                            VECF[:, RINS_OFF:RINS_OFF + 512],
                                            1e12, None, ALU.min)
                        P.run("vector", v_rmin)
                        last_v = P.sig("vector")

                    P.wait("vector", "DS", dBB)

                    def v_cvec(v):
                        v.tensor_mul(VECF[:, 0:512], VECF[:, BB_OFF:BB_OFF + 512],
                                     VECF[:, AC_OFF:AC_OFF + 512])
                        v.drain()
                        v.tensor_mul(VECB[:, C1_OFF:C1_OFF + 512], VECF[:, 0:512],
                                     VECB[:, RK_OFF:RK_OFF + 512])
                        v.tensor_mul(VECB[:, C2_OFF:C2_OFF + 512],
                                     VECB[:, RK_OFF:RK_OFF + 512],
                                     VECB[:, AINV_OFF:AINV_OFF + 512])
                        v.tensor_mul(VECB[:, C3_OFF:C3_OFF + 512],
                                     VECF[:, AC_OFF:AC_OFF + 512],
                                     VECB[:, RQ_OFF:RQ_OFF + 512])
                    P.run("vector", v_cvec)
                    vCV = P.sig("vector")
                    last_v = vCV

                    dRP = {}
                    for half in range(2):
                        P.wait("sync", "V", vCV)
                        if T > 0 or half > 0:
                            P.wait("sync", "T", last_t)
                        for h_ in range(4):
                            P.dma("sync", lambda e, half=half, h_=h_: e.dma_start(
                                out=RP1[0:1, h_ * 1536:h_ * 1536 + 1536].rearrange(
                                    "o (s c) -> o s c", s=6),
                                in_=VECB[32 * h_:32 * h_ + 1, :].rearrange(
                                    "o (s w) -> o s w", s=6)[:, :, 256 * half:256 * half + 256]))
                            P.dma("gpsimd", lambda e, half=half, h_=h_: e.dma_start(
                                out=RP2[0:1, h_ * 512:h_ * 512 + 512].rearrange(
                                    "o (s c) -> o s c", s=2),
                                in_=VECF[32 * h_:32 * h_ + 1, :].rearrange(
                                    "o (s w) -> o s w", s=4)[:, 1:4:2,
                                                            256 * half:256 * half + 256]))
                        dRP[half] = P.cnt["DG"]

                        for c in (2 * half, 2 * half + 1):
                            P.wait("tensor", "DS", P.cnt["DS"])
                            P.wait("tensor", "DG", dRP[half])
                            def t1(t, buf=buf, c=c):
                                for h in range(4):
                                    ks = sils(buf, "k", h, c)
                                    t.matmul(psO1[:, 128 * h:128 * h + 128], ks, ks,
                                             start=True, stop=True)
                                    t.matmul(psG[:, 128 * h:128 * h + 128],
                                             rp1s(3, h, c), rp1s(4, h, c),
                                             start=True, stop=True)
                                    t.matmul(psAB[:, 128 * h:128 * h + 128],
                                             rp1s(4, h, c), rp1s(3, h, c),
                                             start=True, stop=True)
                            TG(t1)

                            def v2a(v):
                                v.tensor_mul(scr(iM), psO1[:], STRICT4[:])
                                v.tensor_mul(scr(iMT), psO1[:], STRICTU4[:])
                                v.drain()
                                v.tensor_mul(scr(iM), scr(iM), psG[:])
                                v.tensor_mul(scr(iMT), scr(iMT), psAB[:])
                                v.drain()
                                v.scalar_tensor_tensor(scr(iZ0), scr(iMT), -1.0,
                                                       IDENT4[:], ALU.mult, ALU.add)
                            VG(v2a)

                            def t3(t, buf=buf, c=c):
                                for h in range(4):
                                    t.matmul(psO1[:, 128 * h:128 * h + 128],
                                             sils(buf, "k", h, c), sils(buf, "q", h, c),
                                             start=True, stop=True)
                                    t.matmul(psG[:, 128 * h:128 * h + 128],
                                             rp1s(4, h, c), rp1s(5, h, c),
                                             start=True, stop=True)
                                    t.matmul(psAB[:, 128 * h:128 * h + 128],
                                             ones_r_bf[:], rp1s(5, h, c),
                                             start=True, stop=True)
                            TG(t3)

                            def v4a(v, buf=buf, c=c):
                                v.tensor_mul(scr(iSCRT), psO1[:], INCL4[:])
                                v.tensor_mul(scr(iQAT).rearrange("p (h w) -> p h w", h=4),
                                             silg(buf, "q", c), ps4(psAB))
                                v.drain()
                                v.tensor_mul(scr(iSCRT), scr(iSCRT), psG[:])
                            VG(v4a)

                            def t5(t, c=c):
                                for h in range(4):
                                    t.matmul(psG[:, 128 * h:128 * h + 128],
                                             ones_r_bf[:], rp2s(1, h, c),
                                             start=True, stop=True)
                                    t.matmul(psO1[:, 128 * h:128 * h + 128],
                                             ones_r_bf[:], rp1s(3, h, c),
                                             start=True, stop=True)
                                    t.matmul(psQ[:, 0:512][:, 128 * h:128 * h + 128],
                                             ones_r_bf[:], rp1s(1, h, c),
                                             start=True, stop=True)
                                    acb = h * (2 * 256) + 0 * 256 + (c % 2) * 128 + 127
                                    t.matmul(psAB[0:1, 128 * h:128 * h + 128],
                                             RP2[0:1, acb:acb + 1],
                                             ones_r_bf[:], start=True, stop=True)
                            TG(t5)

                            def v6a(v, buf=buf, c=c):
                                v.tensor_mul(scr(iFMT).rearrange("p (h w) -> p h w", h=4),
                                             silg(buf, "v", c), ps4(psG))
                                v.tensor_mul(scr(iPP).rearrange("p (h w) -> p h w", h=4),
                                             silg(buf, "k", c), ps4(psO1))
                                v.tensor_mul(scr(iGT).rearrange("p (h w) -> p h w", h=4),
                                             silg(buf, "k", c), ps4(psQ[:, 0:512]))
                                v.tensor_copy(ACROW[0:1, :], psAB[0:1, :])
                            VG(v6a)

                            def t7(t, c=c):
                                for h in range(4):
                                    t.transpose(psTR[:, 128 * h:128 * h + 128],
                                                scr(iFMT)[:, 128 * h:128 * h + 128],
                                                ident[:])
                                    t.transpose(psTR[:, 512 + 128 * h:512 + 128 * h + 128],
                                                scr(iPP)[:, 128 * h:128 * h + 128],
                                                ident[:])
                                    t.transpose(psTR[:, 1024 + 128 * h:1024 + 128 * h + 128],
                                                scr(iGT)[:, 128 * h:128 * h + 128],
                                                ident[:])
                                    t.matmul(psO1[:, 128 * h:128 * h + 128],
                                             rp1s(0, h, c),
                                             ACROW[0:1, 128 * h:128 * h + 128],
                                             start=True, stop=True)
                                    t.matmul(psG[:, 128 * h:128 * h + 128],
                                             ones_r_bf[:],
                                             ACROW[0:1, 128 * h:128 * h + 128],
                                             start=True, stop=True)
                            TG(t7)

                            def v8(v):
                                v.tensor_copy(scr(iRV), psTR[:, 0:512])
                                v.tensor_copy(scr(iRBK), psTR[:, 512:1024])
                                v.tensor_copy(scr(iKT2), psTR[:, 1024:1536])
                                v.tensor_copy(UBCT[:], psO1[:])
                                v.tensor_copy(ACB[:], psG[:])
                            VG(v8)

                            levels = [
                                (iMT, iM, iP0, iP1, True),
                                (iP1, iP0, iM, iMT, True),
                                (iMT, iM, iP0, iP1, True),
                                (iP1, iP0, iM, iMT, True),
                                (iMT, iM, iP0, iP1, True),
                                (iP1, iP0, iM, None, False),
                            ]
                            zc, zn = iZ0, iZ1
                            for li in range(7):
                                sq = levels[li] if li < 6 else None
                                do_chain = li >= 1

                                def t9(t, sq=sq, do_chain=do_chain, zc=zc):
                                    for h in range(4):
                                        sl = slice(128 * h, 128 * h + 128)
                                        if sq is not None:
                                            lt, ln_, dn, dt, both = sq
                                            t.matmul(psQ[:, 0:512][:, sl],
                                                     scr(lt)[:, sl], scr(ln_)[:, sl],
                                                     start=True, stop=True)
                                            if both:
                                                t.matmul(psQ[:, 512:1024][:, sl],
                                                         scr(ln_)[:, sl], scr(lt)[:, sl],
                                                         start=True, stop=True)
                                        if do_chain:
                                            t.matmul(psQ[:, 1024:1536][:, sl],
                                                     scr(iF)[:, sl], scr(zc)[:, sl],
                                                     start=True, stop=True)
                                TG(t9)

                                def v9(v, sq=sq, do_chain=do_chain, zn=zn):
                                    if sq is not None:
                                        lt, ln_, dn, dt, both = sq
                                        v.tensor_copy(scr(dn), psQ[:, 0:512])
                                        if both:
                                            v.tensor_copy(scr(dt), psQ[:, 512:1024])
                                    if do_chain:
                                        v.tensor_copy(scr(zn), psQ[:, 1024:1536])
                                    if sq is not None:
                                        v.drain()
                                        v.tensor_add(scr(iF), scr(dn), IDENT4[:])
                                VG(v9)
                                if do_chain:
                                    zc, zn = zn, zc
                            def t12(t, zc=zc):
                                for h in range(4):
                                    sl = slice(128 * h, 128 * h + 128)
                                    t.matmul(psQ[:, 0:512][:, sl], scr(zc)[:, sl],
                                             scr(iRV)[:, sl], start=True, stop=True)
                                    t.matmul(psQ[:, 512:1024][:, sl],
                                             scr(iRBK)[:, sl], scr(zc)[:, sl],
                                             start=True, stop=True)
                            TG(t12)

                            def v12(v):
                                v.tensor_copy(scr(iPP), psQ[:, 0:512])
                                v.tensor_copy(scr(iGT), psQ[:, 512:1024])
                            VG(v12)

                            def t13(t):
                                for h in range(4):
                                    sl = slice(128 * h, 128 * h + 128)
                                    t.matmul(psO1[:, sl], scr(iGT)[:, sl], StB[:, sl],
                                             start=True, stop=True)
                            TG(t13)

                            def v13(v):
                                v.tensor_tensor(scr(iUU), scr(iPP), psO1[:], ALU.subtract)
                                v.drain()
                                v.tensor_mul(scr(iUT), scr(iUU), UBCT[:])
                            VG(v13)

                            def t14(t):
                                for h in range(4):
                                    sl = slice(128 * h, 128 * h + 128)
                                    t.matmul(psO1[:, sl], scr(iSCRT)[:, sl],
                                             scr(iUU)[:, sl], start=True, stop=False)
                                    t.matmul(psO1[:, sl], scr(iQAT)[:, sl],
                                             StB[:, sl], start=False, stop=True)
                                    t.matmul(psG[:, sl], scr(iKT2)[:, sl],
                                             scr(iUT)[:, sl], start=True, stop=True)
                            TG(t14)

                            def v14(v, c=c):
                                v.tensor_mul(CONVA[:], StF[:], ACB[:])
                                v.drain()
                                v.tensor_add(StF[:], CONVA[:], psG[:])
                                v.drain()
                                v.tensor_copy(StB[:], StF[:])
                                v.tensor_mul(GATED[:], GSIL[:, 512 * c:512 * c + 512],
                                             psO1[:])
                            VG(v14)

                            P.wait("sync", "V", last_v)
                            P.dma("sync", lambda e, T=T, c=c: e.dma_start(
                                out=ag2_in[512 * T + 128 * c:512 * T + 128 * c + 128, :],
                                in_=GATED[:]))
                dAG2w = P.cnt["DS"]

            P.wait("gpsimd", "DS", dAG2w)
            ccAG2 = P.cc(lambda g: g.collective_compute(
                "AllGather", ALU.bypass,
                replica_groups=[[0, 1, 2, 3], [4, 5, 6, 7]],
                ins=[ag2_in], outs=[ag2_out]))

            X2 = nc.alloc_sbuf_tensor_at("X2", [128, NK * TOK], F32, offset=SB_BASE + 0)
            XB2 = nc.alloc_sbuf_tensor_at("XB2", [128, NK * TOK], BF16, offset=SB_BASE + 65536)
            OUTF = nc.alloc_sbuf_tensor_at("OUTF", [128, NK * TOK], BF16, offset=SB_BASE + 98304)
            WOT = nc.alloc_sbuf_tensor_at("WOT", [128, 16 * 1024], BF16, offset=SB_BASE + 131072)
            WGAM2 = nc.alloc_sbuf_tensor_at("WGAM2", [128, NK * D], BF16, offset=SB_BASE + 98304)

            P.wait("sync", "CC", ccAG2)
            dXM2 = P.dma("sync", lambda e: e.dma_start(
                out=X2[:].rearrange("p (k t) -> p k t", k=NK),
                in_=xmid_scr.rearrange("(k p) t -> p k t", p=128)))
            P.wait("scalar", "CC", ccAG2)
            _offreg = []
            dOF = 0
            for j in range(4):
                for m_ in range(4):
                    e_ = 4 * j + m_

                    def s_tr(e, j=j, m_=m_, e_=e_):
                        if not _offreg:
                            _offreg.append(e.alloc_register("coreoff_r"))
                            _offreg.append(e.alloc_register("coreoff_r2"))
                            e.reg_load(_offreg[0], coreoff_sb[0:1, 0:1])
                        e.reg_add(_offreg[1], _offreg[0],
                                  j * (NSEQ * 512) + 128 * m_)
                        src = bass.AP(ag2_out.tensor, _offreg[1],
                                      [[512, 1024], [1, 128]])
                        return e.dma_start_transpose(
                            OUTF[:, TOK * e_:TOK * e_ + TOK], src)
                    dOF = P.dma("scalar", s_tr)
            dWO = {}
            dWO[0] = P.dma("gpsimd", lambda e: e.dma_start(
                out=WOT[:].rearrange("p (k o) -> p k o", k=NK),
                in_=woT.rearrange("(k p) o -> p k o", p=128)[:, :, 0:1024]))

            with nc.psum_tensor("psW", [128, 3 * 512], F32) as psW:
                vW = 0
                v_sigs = []
                it = 0
                for half in range(2):
                    if half == 1:
                        P.wait("gpsimd", "T", last_t)
                        dWO[1] = P.dma("gpsimd", lambda e: e.dma_start(
                            out=WOT[:].rearrange("p (k o) -> p k o", k=NK),
                            in_=woT.rearrange("(k p) o -> p k o", p=128)[:, :, 1024:2048]))
                    for oc8 in range(8):
                        oc = 8 * half + oc8
                        for t2 in range(2):
                            bank = it % 3
                            if half == 0 and oc8 == 0 and t2 == 0:
                                P.wait("tensor", "DA", dOF)
                                P.wait("tensor", "DG", dWO[0])
                                P.wait("tensor", "DS", dXM2)
                            if half == 1 and oc8 == 0 and t2 == 0:
                                P.wait("tensor", "DG", dWO[1])
                            if it >= 3:
                                P.wait("tensor", "V", v_sigs[it - 3])

                            def mm_w(t, oc8=oc8, t2=t2, bank=bank):
                                for k in range(16):
                                    t.matmul(psW[:, 512 * bank:512 * bank + 512],
                                             WOT[:, 1024 * k + 128 * oc8:1024 * k + 128 * oc8 + 128],
                                             OUTF[:, TOK * k + 512 * t2:TOK * k + 512 * t2 + 512],
                                             start=(k == 0), stop=(k == 15))
                            P.run("tensor", mm_w)
                            last_t = P.sig("tensor")
                            P.wait("vector", "T", last_t)

                            def v_wev(v, oc=oc, t2=t2, bank=bank):
                                xs = X2[:, TOK * oc + 512 * t2:TOK * oc + 512 * t2 + 512]
                                v.tensor_add(xs, xs, psW[:, 512 * bank:512 * bank + 512])
                            P.run("vector", v_wev)
                            vW = P.sig("vector")
                            v_sigs.append(vW)
                            it += 1
            vATT = vW

            P.wait("gpsimd", "V", vATT)
            dWG2 = P.dma("gpsimd", lambda e: e.dma_start(
                out=WGAM2[:].rearrange("p (k o) -> p k o", k=NK),
                in_=wgam_scr.rearrange("(k p) o -> p k o", p=128)))

            vxb3 = {}
            for k in range(16):
                if k == 0:
                    P.wait("vector", "V", vATT)

                def v_xb3(v, k=k):
                    v.tensor_copy(XB2[:, TOK * k:TOK * k + TOK], X2[:, TOK * k:TOK * k + TOK])
                P.run("vector", v_xb3)
                vxb3[k] = P.sig("vector")

            sq_waits3 = {k: vATT for k in range(16)}
            vBC3 = ln_stats("3", "V", sq_waits3,
                            lambda k, j: XB2[:, TOK * k + 512 * j:TOK * k + 512 * j + 512],
                            inner_ln=True, sum_wait=("V", vxb3), guard_v=vATT,
                            xsrc=X2)
            P.wait("tensor", "DG", dWG2)
            P.wait("tensor", "DS", vBC3)
            vFFN2 = ffn("2", wgam=WGAM2, xb=XB2, xdst=X2)

            P.wait("sync", "V", vFFN2)
            for k in range(16):
                P.dma("sync", lambda e, k=k: e.dma_start(
                    out=y_fm[128 * k:128 * k + 128, :], in_=X2[:, TOK * k:TOK * k + TOK]))
            dOUT = P.cnt["DS"]
            P.wait("sync", "DS", dOUT)
            P.wait("gpsimd", "DS", dOUT)
            P.wait("vector", "DS", dOUT)
            P._complete = True

        except _StopRecord:
            pass
        with nc.allow_low_precision("bf16 intermediates; tolerance calibrated "
                                    "for bf16 matmuls with fp32 accumulation"):
            if not getattr(P, "_complete", False):
                _emit_tail(nc, P, stack, X, y_fm, TOK)
            else:
                blk = stack.enter_context(nc.Block())
                P.emit(blk)
    return nc


def _emit_tail(nc, P, stack, X, y_fm, TOK):
    P.run("vector", lambda v: v.memset(X[:, 0:TOK], 0.0))
    vZ = P.sig("vector")
    P.wait("sync", "V", vZ)
    for k in range(16):
        P.dma("sync", lambda e, k=k: e.dma_start(
            out=y_fm[128 * k:128 * k + 128, :], in_=X[:, 0:TOK]))
    dOUT = P.cnt["DS"]
    P.wait("sync", "DS", dOUT)
    P.wait("gpsimd", "DS", dOUT)

    blk = stack.enter_context(nc.Block())
    P.emit(blk)

    return nc


def _prep_inputs(L0_attn_x, L0_attn_W_qkv, L0_attn_conv_w, L0_attn_W_ab,
                 L0_attn_b_ab, L0_attn_W_g, L0_attn_W_o, L0_ffn_weight,
                 L0_ffn_bias, L0_ffn_ln_gamma, L0_ffn_ln_beta):
    asc = np.ascontiguousarray
    f32 = np.float32
    in_maps = []
    wfT = asc(np.asarray(L0_ffn_weight, f32).T)
    woT = asc(np.asarray(L0_attn_W_o, f32).T)
    gcols = asc(np.asarray(L0_ffn_ln_gamma, f32).reshape(16, 128).T)
    bcols = asc(np.asarray(L0_ffn_ln_beta, f32).reshape(16, 128).T)
    fbcols = asc(np.asarray(L0_ffn_bias, f32).reshape(16, 128).T)
    Wqkv = np.asarray(L0_attn_W_qkv, f32)
    Wab = np.asarray(L0_attn_W_ab, f32)
    bab = np.asarray(L0_attn_b_ab, f32)
    Wg = np.asarray(L0_attn_W_g, f32)
    cw = np.asarray(L0_attn_conv_w, f32)
    x = np.asarray(L0_attn_x, f32)
    for c in range(8):
        g, r = divmod(c, 4)
        h0 = 4 * r
        rows_q = slice(128 * h0, 128 * h0 + 512)
        rows_k = slice(2048 + 128 * h0, 2048 + 128 * h0 + 512)
        rows_v = slice(4096 + 128 * h0, 4096 + 128 * h0 + 512)
        wqkv_rows = np.concatenate([Wqkv[rows_q], Wqkv[rows_k], Wqkv[rows_v]], 0)
        wqkvT = asc(wqkv_rows.T)
        wabsp = np.zeros((2048, 128), f32)
        babcol = np.zeros((128, 1), f32)
        for h in range(4):
            wabsp[:, 32 * h] = Wab[h0 + h]
            wabsp[:, 32 * h + 16] = Wab[16 + h0 + h]
            babcol[32 * h, 0] = bab[h0 + h]
            babcol[32 * h + 16, 0] = bab[16 + h0 + h]
        wgT = asc(Wg[128 * h0:128 * h0 + 512].T)
        ch_rows = np.concatenate(
            [cw[rows_q], cw[rows_k], cw[rows_v]], 0)
        cw48 = asc(ch_rows.reshape(12, 128, 4).transpose(1, 0, 2).reshape(128, 48))
        xT = asc(x[g, 1024 * r:1024 * r + 1024].T)
        coreoff = np.array([[1024 * r * 512]], np.int32)
        in_maps.append(dict(
            xT=xT, wfT=wfT, wqkvT=wqkvT, wabsp=wabsp, wgT=wgT, woT=woT,
            cw48=cw48, gcols_d=gcols, bcols_d=bcols, fbcols_d=fbcols,
            babcol_d=babcol, coreoff_d=coreoff))
    return in_maps


_nc_cache = {}


def kernel(**inputs):
    in_maps = _prep_inputs(**inputs)
    key = "dbg" if DEBUG else "plain"
    if key not in _nc_cache:
        _nc_cache[key] = build_program(core_dbg=DEBUG)
    nc = _nc_cache[key]
    res = run_bass_kernel_spmd(nc, in_maps, core_ids=list(range(8)))
    out = np.empty((2, 4096, 2048), np.float32)
    for c in range(8):
        g, r = divmod(c, 4)
        out[g, 1024 * r:1024 * r + 1024, :] = res.results[c]["y_fm"].T
    if DEBUG:
        kernel.debug_results = res.results
    return out
